# version 06
# baseline (speedup 1.0000x reference)
from contextlib import ExitStack
import numpy as np
import ml_dtypes
import concourse.bass as bass
import concourse.mybir as mybir
from concourse.bass_utils import run_bass_kernel_spmd

F32 = mybir.dt.float32
BF16 = mybir.dt.bfloat16
AF = mybir.ActivationFunctionType
ALU = mybir.AluOpType
AX = mybir.AxisListType

NCORE = 8
B, S, D = 4, 4096, 1024
NBLK, BLK = 32, 128
NKC = D // 128
LOCT = 2048
GT = B * S
EPS = 1e-5
NEG = -1e30
ENG = ("tensor", "vector", "scalar", "gpsimd", "sync")
V, T, Sc, G, Y = "vector", "tensor", "scalar", "gpsimd", "sync"

bf = ml_dtypes.bfloat16


def owned_blocks(c):
    return sorted([c, 15 - c, 16 + c, 31 - c])



def prep_wT(w):
    o, i = w.shape
    t = np.transpose(w).reshape(i // 128, 128, o)
    return np.ascontiguousarray(t.transpose(1, 0, 2).reshape(128, -1), dtype=np.float32)


def prep_bias(b_):
    o = b_.shape[0]
    return np.ascontiguousarray(b_.reshape(o // 128, 128).T, dtype=np.float32)


def shard_x(xfull, c):
    ob = owned_blocks(c)
    xb = np.asarray(xfull, np.float32).reshape(B, NBLK, BLK, NKC, 128)
    sl = xb[:, ob]
    return np.ascontiguousarray(sl.transpose(4, 3, 0, 1, 2).reshape(128, NKC * LOCT))


def unshard_x(parts):
    out = np.empty((B, NBLK, BLK, NKC, 128), np.float32)
    for c in range(NCORE):
        sl = np.asarray(parts[c]).reshape(128, NKC, B, 4, BLK).transpose(2, 3, 4, 1, 0)
        out[:, owned_blocks(c)] = sl
    return out.reshape(B, S, D)


def x_global(xfull):
    xb = np.asarray(xfull, np.float32).reshape(GT, NKC, 128)
    return np.ascontiguousarray(xb.transpose(2, 1, 0).reshape(128, NKC * GT))


def rope_tables(positions, dtype=np.float32):
    pos = np.asarray(positions, np.float64)
    inv = 1.0 / 10000.0 ** (np.arange(0, 64, 2) / 64.0)
    ang = pos[None, :] * inv[:, None]
    cos, sin = np.cos(ang), np.sin(ang)
    cos64 = np.concatenate([cos, cos], 0)
    sgn64 = np.concatenate([-sin, sin], 0)
    c2 = np.concatenate([cos64, cos64], 0)
    s2 = np.concatenate([sgn64, sgn64], 0)
    return c2.astype(dtype), s2.astype(dtype)


def local_positions(c):
    ob = owned_blocks(c)
    pos = np.empty(LOCT, np.int64)
    for t in range(16):
        b_, j = divmod(t, 4)
        pos[t * 128:(t + 1) * 128] = ob[j] * 128 + np.arange(128)
    return pos


def global_positions():
    pos = np.empty(GT, np.int64)
    for b_ in range(B):
        for blk in range(NBLK):
            g0 = (b_ * NBLK + blk) * 128
            pos[g0:g0 + 128] = blk * 128 + np.arange(128)
    return pos


def build_skj(rpb, c):
    ob = owned_blocks(c)
    n = rpb.shape[0]
    p = np.arange(128)
    kt = np.arange(NBLK)
    qq = np.arange(128)
    out = np.empty((16, 4, 128, NBLK * 128), np.float32)
    key = kt[:, None, None] * 128 + p[None, :, None]
    for j, bj in enumerate(ob):
        q = bj * 128 + qq[None, None, :]
        rel = key - q
        idx = np.clip(rel + 4095, 0, n - 1)
        vals = rpb[idx, :]
        vals = np.where((rel > 0)[..., None], np.float32(NEG), vals)
        out[:, j] = vals.transpose(3, 1, 0, 2).reshape(16, 128, NBLK * 128)
    return np.ascontiguousarray(out)



class Sem:
    __slots__ = ("h", "n")

    def __init__(self, h):
        self.h = h
        self.n = 0


class Bld:
    def __init__(self, nc, ctx):
        self.nc = nc
        self.ctx = ctx
        self.q = {e: [] for e in ENG}
        self._ns = 0
        self.bar_sem = self.sem("bar")

    def sem(self, name=None):
        self._ns += 1
        h = self.ctx.enter_context(self.nc.semaphore(name or f"s{self._ns}"))
        return Sem(h)

    def op(self, eng, meth, *a, inc=(), **kw):
        if isinstance(inc, Sem):
            inc = ((inc, 1),)
        elif inc and isinstance(inc[0], Sem):
            inc = (inc,)
        for s, k in inc:
            s.n += k
        self.q[eng].append(("op", meth, a, kw, tuple((s.h, k) for s, k in inc)))

    def sop(self, eng, meth, *a, inc=(), **kw):
        if not hasattr(self, "_chain"):
            self._chain = {e: self.sem(f"ch_{e}") for e in ENG}
        cs = self._chain[eng]
        if isinstance(inc, Sem):
            inc = ((inc, 1),)
        elif inc and isinstance(inc[0], Sem):
            inc = (inc,)
        self.op(eng, meth, *a, inc=(cs, 1), **kw)
        self.wait(eng, cs)
        for sm, k in inc:
            sm.n += k
            self.q[eng].append(("op", "sem_inc", (sm.h, k), {}, ()))

    def wait(self, eng, sem, val=None):
        v = sem.n if val is None else val
        assert v <= sem.n, "wait target exceeds emission count (would hang)"
        if v > 0:
            self.q[eng].append(("wait", sem.h, v))

    def barrier(self, engines=ENG):
        if not hasattr(self, "_dummy"):
            self._dummy = self.ctx.enter_context(
                self.nc.sbuf_tensor("bar_dummy", [1, 8], mybir.dt.float32))
        if "vector" in engines:
            self.sop("vector", "memset", self._dummy[0:1, 0:1], 0.0)
        if "scalar" in engines:
            self.sop("scalar", "memzero", self._dummy[0:1, 1:2])
        if "gpsimd" in engines:
            self.sop("gpsimd", "memset", self._dummy[0:1, 2:3], 0.0)
        s = self.bar_sem
        for e in engines:
            self.op(e, "nop", inc=(s, 1))
        t = s.n
        for e in engines:
            self.wait(e, s, t)

    def finalize(self):
        with self.nc.Block() as blk:
            for e in ENG:
                def body(h, eng=e):
                    for item in self.q[eng]:
                        if item[0] == "wait":
                            h.wait_ge(item[1], item[2])
                        else:
                            _, meth, a, kw, incs = item
                            inst = getattr(h, meth)(*a, **kw)
                            for sh, k in incs:
                                inst.then_inc(sh, k)
                getattr(blk, e)(body)


class Env:
    pass



def emit_replicate(bld, env, row_ap, dest_ap, nrows=128):
    w = row_ap.shape[-1]
    bld.wait(T, env.ev)
    bld.op(T, "matmul", out=env.ps_rep[0:nrows, 0:w],
           lhsT=env.ones_row_f32[0:1, 0:nrows], rhs=row_ap,
           start=True, stop=True, inc=env.mm)
    bld.wait(V, env.mm)
    bld.op(V, "tensor_copy", out=dest_ap, in_=env.ps_rep[0:nrows, 0:w], inc=env.ev)


def emit_ln_stats(bld, env, xsrc, nkc, ncols, mu_row, rstd_row):
    nfeat = float(nkc * 128)
    bld.wait(T, env.ev)
    bld.wait(Sc, env.ev)
    for kc in range(nkc):
        bld.op(T, "matmul", out=env.ps_mean[0:1, 0:ncols],
               lhsT=env.ones_col_f32, rhs=xsrc(kc),
               start=(kc == 0), stop=(kc == nkc - 1),
               inc=(env.mm if kc == nkc - 1 else ()))
    sq_mm = []
    for kc in range(nkc):
        if kc >= 1:
            bld.wait(Sc, env.mm, sq_mm[kc - 1])
        bld.op(Sc, "activation", out=env.sq_scr[:, 0:ncols], in_=xsrc(kc),
               func=AF.Square, inc=env.sc)
        bld.wait(T, env.sc, env.sc.n)
        bld.op(T, "matmul", out=env.ps_sq[0:1, 0:ncols],
               lhsT=env.ones_col_f32, rhs=env.sq_scr[:, 0:ncols],
               start=(kc == 0), stop=(kc == nkc - 1), inc=env.mm)
        sq_mm.append(env.mm.n)
    bld.wait(V, env.mm)
    bld.sop(V, "tensor_scalar_mul", mu_row, env.ps_mean[0:1, 0:ncols], 1.0 / nfeat)
    bld.sop(V, "tensor_scalar_mul", env.row_tmp[0:1, 0:ncols],
            env.ps_sq[0:1, 0:ncols], 1.0 / nfeat)
    bld.sop(V, "tensor_mul", env.row_tmp2[0:1, 0:ncols], mu_row, mu_row)
    bld.sop(V, "tensor_sub", env.row_tmp[0:1, 0:ncols],
            env.row_tmp[0:1, 0:ncols], env.row_tmp2[0:1, 0:ncols])
    bld.sop(V, "tensor_scalar_add", env.row_tmp[0:1, 0:ncols],
            env.row_tmp[0:1, 0:ncols], EPS)
    bld.sop(V, "reciprocal", env.row_tmp2[0:1, 0:ncols],
            env.row_tmp[0:1, 0:ncols], inc=env.ev)
    bld.wait(Sc, env.ev)
    bld.sop(Sc, "activation", out=rstd_row, in_=env.row_tmp2[0:1, 0:ncols],
            func=AF.Sqrt, inc=env.sc)


def emit_ln_apply(bld, env, xsrc, nkc, ncols, xn_dst, gain_bias=None):
    for kc in range(nkc):
        if gain_bias is not None:
            bld.wait(V, env.sc)
        bld.op(V, "tensor_sub", env.vec_scr[:, 0:ncols], xsrc(kc), env.murep[:, 0:ncols])
        if gain_bias is None:
            bld.op(V, "tensor_mul", xn_dst(kc), env.vec_scr[:, 0:ncols],
                   env.rstdrep[:, 0:ncols], inc=env.ev)
        else:
            g_ap, b_ap = gain_bias(kc)
            bld.op(V, "tensor_mul", env.vec_scr[:, 0:ncols], env.vec_scr[:, 0:ncols],
                   env.rstdrep[:, 0:ncols], inc=env.ev)
            bld.wait(Sc, env.ev)
            bld.op(Sc, "activation", out=xn_dst(kc), in_=env.vec_scr[:, 0:ncols],
                   func=AF.Identity, bias=b_ap, scale=g_ap, inc=env.sc)


def emit_ln_group(bld, env, xsrc, nkc, ncols, mu_row, rstd_row, xn_dst=None,
                  gain_bias=None):
    emit_ln_stats(bld, env, xsrc, nkc, ncols, mu_row, rstd_row)
    if xn_dst is None:
        return
    bld.wait(V, env.sc)
    emit_replicate(bld, env, mu_row, env.murep[:, 0:ncols])
    emit_replicate(bld, env, rstd_row, env.rstdrep[:, 0:ncols])
    emit_ln_apply(bld, env, xsrc, nkc, ncols, xn_dst, gain_bias)


def emit_load_cast(bld, env, dram_cols, stage, dest, total_cols, chunk):
    bld.wait(Y, env.gcast)
    done = []
    for i in range((total_cols + chunk - 1) // chunk):
        c0, c1 = i * chunk, min((i + 1) * chunk, total_cols)
        par = i % 2
        if i >= 2:
            bld.wait(Y, env.gcast, done[i - 2])
        bld.op(Y, "dma_start", out=stage[par][:, 0:c1 - c0], in_=dram_cols(c0, c1),
               inc=(env.ld, 16))
        bld.wait(G, env.ld)
        bld.op(G, "tensor_copy", out=dest[:, c0:c1], in_=stage[par][:, 0:c1 - c0],
               inc=env.gcast)
        done.append(env.gcast.n)



def emit_kv_prologue(bld, env, cfg):
    nh_per_chunk = 128 // cfg.dh
    n_tg = cfg.n_tg
    local = getattr(cfg, "local", False)
    for tg in range(n_tg):
        gt0 = tg * 512
        bld.wait(Y, env.ev)
        bld.wait(Y, env.sc)
        bld.wait(Y, env.mm)
        if not local:
            bld.op(Y, "dma_start", out=env.XG.rearrange("p (k c) -> p k c", k=NKC),
                   in_=cfg.xsrc_dram.rearrange("p (k t) -> p k t", k=NKC)[:, :, gt0:gt0 + 512],
                   inc=(env.ld, 16))
        if cfg.rope:
            bld.op(Y, "dma_start", out=env.rg_c[:, :], in_=cfg.rc_glob[:, gt0:gt0 + 512],
                   inc=(env.ld, 16))
            bld.op(Y, "dma_start", out=env.rg_s[:, :], in_=cfg.rs_glob[:, gt0:gt0 + 512],
                   inc=(env.ld, 16))
        bld.wait(T, env.ld)
        bld.wait(Sc, env.ld)
        bld.wait(V, env.ld)
        cfg.xn_maker(bld, env, tg)
        for oc in range(NKC):
            bank = env.ps[oc % 2]
            bld.wait(T, env.ev)
            bld.wait(T, env.sc)
            for kc in range(NKC):
                bld.op(T, "matmul", out=bank[:, :],
                       lhsT=cfg.WKB[:, kc * D + oc * 128: kc * D + (oc + 1) * 128],
                       rhs=env.xng[:, kc * 512:(kc + 1) * 512],
                       start=(kc == 0), stop=(kc == NKC - 1),
                       inc=(env.mm if kc == NKC - 1 else ()))
            kbf = cfg.kbf
            bld.wait(Sc, env.mm)
            if cfg.rope:
                bld.wait(Sc, env.ev)
                bld.op(Sc, "activation", out=env.kev[:, :], in_=bank[:, :],
                       func=AF.Identity, bias=cfg.t_kb[:, oc:oc + 1], inc=env.sc)
                bld.wait(G, env.sc)
                bld.wait(G, env.ev)
                for half in range(4):
                    p0 = half * 32
                    srcp = (p0 + 32) % 64 + (64 if half >= 2 else 0)
                    bld.op(G, "tensor_copy", out=env.kev2[p0:p0 + 32, :],
                           in_=env.kev[srcp:srcp + 32, :],
                           inc=(env.gcast if half == 3 else ()))
                bld.wait(V, env.gcast)
                bld.wait(V, env.gc, cfg.kstore.get(0, 0))
                bld.op(V, "tensor_mul", env.vec_scr[:, :], env.kev2[:, :], env.rg_s[:, :])
                bld.op(V, "tensor_mul", env.kev2[:, :], env.kev[:, :], env.rg_c[:, :])
                bld.op(V, "tensor_add", kbf[:, :], env.kev2[:, :], env.vec_scr[:, :],
                       inc=env.ev)
            else:
                bld.wait(Sc, env.gc, cfg.kstore.get(0, 0))
                bld.op(Sc, "activation", out=kbf[:, :], in_=bank[:, :],
                       func=AF.Identity, bias=cfg.t_kb[:, oc:oc + 1], inc=env.sc)
                bld.wait(Sc, env.sc, env.sc.n)
            if cfg.rope:
                bld.wait(Sc, env.ev)
            bld.op(Sc, "activation", out=env.sqb[:, :], in_=kbf[:, :], func=AF.Square,
                   inc=env.sc)
            bld.wait(T, env.sc)
            if nh_per_chunk == 2:
                bld.op(T, "matmul", out=env.ps_mean[0:1, :], lhsT=cfg.onesh[:, 0:1],
                       rhs=env.sqb[:, :], start=True, stop=True)
                bld.op(T, "matmul", out=env.ps_sq[0:1, :], lhsT=cfg.onesh[:, 1:2],
                       rhs=env.sqb[:, :], start=True, stop=True, inc=env.mm)
                bld.wait(V, env.mm)
                for pp in (env.ps_mean, env.ps_sq):
                    bld.sop(V, "reduce_max", env.sc11[0:1, 0:1], pp[0:1, :], axis=AX.X)
                    bld.sop(V, "tensor_max", cfg.kn2max[:, :], cfg.kn2max[:, :],
                            env.sc11[0:1, 0:1], inc=env.ev)
            else:
                bld.op(T, "matmul", out=env.ps_mean[0:1, :], lhsT=cfg.ones_col_bf,
                       rhs=env.sqb[:, :], start=True, stop=True, inc=env.mm)
                bld.wait(V, env.mm)
                bld.sop(V, "reduce_max", env.sc11[0:1, 0:1], env.ps_mean[0:1, :], axis=AX.X)
                bld.sop(V, "tensor_max", cfg.kn2max[:, :], cfg.kn2max[:, :],
                        env.sc11[0:1, 0:1], inc=env.ev)
            bld.wait(G, env.ev)
            bld.wait(G, env.sc)
            bld.op(G, "dma_start",
                   out=cfg.kdram[:, oc * cfg.kd_t + gt0: oc * cfg.kd_t + gt0 + 512],
                   in_=kbf[:, :], inc=(env.gc, 16))
            cfg.kstore[0] = env.gc.n
        for tile in range(4):
            t0 = tile * 128
            for vh in range(2):
                bank = env.ps[2 + (tile * 2 + vh) % 2]
                bld.wait(T, env.ev)
                bld.op(T, "matmul", out=bank[:, :], lhsT=env.ones_row_f32,
                       rhs=cfg.t_vb[0:1, vh * 512:(vh + 1) * 512], start=True, stop=False,
                       skip_group_check=True)
                for kc in range(NKC):
                    bld.op(T, "matmul", out=bank[:, :],
                           lhsT=env.xng[:, kc * 512 + t0: kc * 512 + t0 + 128],
                           rhs=cfg.WVB[:, kc * D + vh * 512: kc * D + (vh + 1) * 512],
                           start=False, stop=(kc == NKC - 1), skip_group_check=True,
                           inc=(env.mm if kc == NKC - 1 else ()))
                vbf = cfg.vbf
                bld.wait(V, env.mm)
                bld.wait(V, env.gc, cfg.vstore.get(0, 0))
                bld.op(V, "tensor_copy", out=vbf[:, :], in_=bank[:, :], inc=env.ev)
                bld.wait(G, env.ev)
                nh2 = 512 // cfg.dh
                src = vbf.rearrange("p (h e) -> p h e", h=nh2)
                blk = tg * 4 + tile
                dst = cfg.v2[vh * nh2:(vh + 1) * nh2, blk:blk + 1, :, :].squeeze(1).transpose([1, 0, 2])
                bld.op(G, "dma_start", out=dst, in_=src, inc=(env.gc, 16))
                cfg.vstore[0] = env.gc.n
    bld.barrier()



def emit_attention(bld, env, cfg):
    dh = cfg.dh
    nh_per_chunk = 128 // dh
    estride = dh + 1 if cfg.l_ones else dh
    qmm_end = {}
    ts_end = {}
    prev_es = 0
    bld.wait(Y, env.gc)
    bld.wait(G, env.gc)
    for h in range(cfg.heads):
        hp = (h % nh_per_chunk) * dh
        hc = h // nh_per_chunk
        par_h = h % 2
        bld.wait(Y, env.gcast)
        bld.op(Y, "dma_start",
               out=env.wq_stage[:, 0:NKC * dh].rearrange("p (k c) -> p k c", c=dh),
               in_=cfg.wqT.rearrange("p (k c) -> p k c", c=D)[:, :, h * dh:(h + 1) * dh],
               inc=(env.ld, 16))
        bld.wait(G, env.ld)
        bld.wait(G, env.mm, qmm_end.get(h - 2, 0))
        bld.op(G, "tensor_copy", out=env.WQ[par_h][:, 0:NKC * dh],
               in_=env.wq_stage[:, 0:NKC * dh], inc=env.gcast)
        if cfg.skj is not None:
            skj_done = []
            for i6 in range(16):
                j, q4 = divmod(i6, 4)
                c0 = q4 * 1024
                par = i6 % 2
                if i6 >= 2:
                    bld.wait(Y, env.gcast, skj_done[i6 - 2])
                if i6 == 0:
                    bld.wait(G, env.ts, ts_end.get(h - 1, 0))
                bld.op(Y, "dma_start", out=env.skj_stage[par][:, :],
                       in_=cfg.skj[h:h + 1, j:j + 1, :, c0:c0 + 1024].squeeze(1).squeeze(0),
                       inc=(env.ld, 16))
                bld.wait(G, env.ld)
                bld.op(G, "tensor_copy",
                       out=env.SKJH[:, j * 4096 + c0: j * 4096 + c0 + 1024],
                       in_=env.skj_stage[par][:, :], inc=env.gcast)
                skj_done.append(env.gcast.n)
        for b_ in range(4):
            b0 = b_ * 512
            kpar = 0
            bld.wait(Y, env.ts)
            if dh == 64:
                bld.op(Y, "dma_start", out=env.KBuf[0:64, :],
                       in_=cfg.kdram[hp:hp + 64,
                                     hc * cfg.kd_t + b_ * S: hc * cfg.kd_t + b_ * S + S],
                       inc=(env.ld2, 16))
            else:
                bld.op(Y, "dma_start", out=env.KBuf[:, :],
                       in_=cfg.kdram[:, h * cfg.kd_t + b_ * S: h * cfg.kd_t + b_ * S + S],
                       inc=(env.ld2, 16))
            vpar = b_ % 2 if cfg.vp_bufs == 2 else 0
            vp0 = vpar * NBLK * estride
            bld.wait(G, env.pv)
            for q4 in range(4):
                src = cfg.v2[h:h + 1, b_ * NBLK + q4 * 8:(b_ * NBLK) + (q4 + 1) * 8, :, :] \
                    .squeeze(0).transpose([1, 0, 2])
                dst = env.VP[:, vp0 + q4 * 8 * estride: vp0 + (q4 + 1) * 8 * estride] \
                    .rearrange("p (k e) -> p k e", e=estride)[:, :, 0:dh]
                bld.op(G, "dma_start", out=dst, in_=src, inc=(env.gc, 16))
            bld.wait(T, env.gcast)
            bld.wait(T, env.ev)
            for kc in range(NKC):
                bld.op(T, "matmul", out=env.psAux[0:dh, :],
                       lhsT=env.WQ[par_h][:, kc * dh:(kc + 1) * dh],
                       rhs=env.XN[:, kc * LOCT + b0: kc * LOCT + b0 + 512],
                       start=(kc == 0), stop=(kc == NKC - 1),
                       inc=(env.mm if kc == NKC - 1 else ()))
            if b_ == 3:
                qmm_end[h] = env.mm.n
            bld.wait(Sc, env.mm)
            qtarget = env.qf[0:dh, :] if cfg.rope else env.qbh[0:dh, :]
            bld.op(Sc, "activation", out=qtarget, in_=env.psAux[0:dh, :],
                   func=AF.Identity, bias=cfg.t_qb[hp:hp + dh, hc:hc + 1],
                   scale=cfg.scale, inc=env.sc)
            if getattr(cfg, "qf_dbg", None) is not None and h == 0:
                bld.wait(G, env.sc)
                bld.op(G, "dma_start", out=cfg.qf_dbg[0:dh, b0:b0 + 512],
                       in_=env.qf[0:dh, :] if cfg.rope else env.qbh[0:dh, :], inc=(env.gc, 16))
                if b_ == 0:
                    bld.op(G, "dma_start", out=cfg.wq_dbg[:, 0:NKC * dh],
                           in_=env.WQ[par_h][:, 0:NKC * dh], inc=(env.gc, 16))
                bld.wait(Sc, env.gc)
            if cfg.rope:
                bld.wait(G, env.sc)
                bld.wait(G, env.ev)
                for half in range(2):
                    p0 = half * 32
                    srcp = (p0 + 32) % 64
                    bld.op(G, "tensor_copy", out=env.qscr[p0:p0 + 32, :],
                           in_=env.qf[srcp:srcp + 32, :],
                           inc=(env.gcast if half == 1 else ()))
                bld.wait(V, env.gcast)
                bld.op(V, "tensor_mul", env.vec_scr[0:64, 0:512], env.qscr[0:64, :],
                       cfg.rloc_s[0:64, b0:b0 + 512])
                bld.op(V, "tensor_mul", env.qscr[0:64, :], env.qf[0:64, :],
                       cfg.rloc_c[0:64, b0:b0 + 512])
                bld.op(V, "tensor_add", env.qbh[0:64, :], env.qscr[0:64, :],
                       env.vec_scr[0:64, 0:512], inc=env.ev)
                bld.wait(Sc, env.ev)
            bld.op(Sc, "activation", out=env.sqq[0:dh, :], in_=env.qbh[0:dh, :],
                   func=AF.Square, inc=env.sc)
            bld.wait(T, env.sc)
            bld.op(T, "matmul", out=env.psAux[0:1, :], lhsT=cfg.ones_col_bf[0:dh, :],
                   rhs=env.sqq[0:dh, :], start=True, stop=True, inc=env.mm)
            bld.wait(V, env.mm)
            bld.sop(V, "reduce_max", env.sc11[0:1, 1:2], env.psAux[0:1, :], axis=AX.X)
            bld.sop(V, "tensor_mul", env.sc11[0:1, 2:3], env.sc11[0:1, 1:2],
                    cfg.kn2max[0:1, 0:1], inc=env.ev)
            bld.wait(Sc, env.ev)
            bld.sop(Sc, "activation", out=env.sc11[0:1, 3:4], in_=env.sc11[0:1, 2:3],
                    func=AF.Sqrt, inc=env.sc)
            bld.wait(V, env.sc)
            if cfg.bmax_row is not None:
                bld.sop(V, "tensor_add", env.sc11[0:1, 3:4], env.sc11[0:1, 3:4],
                        cfg.bmax_row[0:1, h:h + 1])
            bld.sop(V, "tensor_scalar_mul", env.sc11[0:1, 4:5], env.sc11[0:1, 3:4], -1.0)
            bld.sop(V, "tensor_copy", env.sc11[0:1, 5:6], env.sc11[0:1, 4:5], inc=env.ev)
            bld.wait(T, env.ev)
            bld.op(T, "matmul", out=env.psAux[:, 0:2], lhsT=env.ones_row_f32,
                   rhs=env.sc11[0:1, 4:6], start=True, stop=True, inc=env.mm)
            bld.wait(V, env.mm)
            bld.wait(V, env.es, prev_es)
            bld.op(V, "tensor_copy", out=env.negcrep[:, :], in_=env.psAux[:, 0:2],
                   inc=env.ev)
            if getattr(cfg, "q_dbg", None) is not None and h == 0:
                bld.wait(G, env.ev)
                bld.op(G, "dma_start", out=cfg.q_dbg[0:dh, b0:b0 + 512],
                       in_=env.qbh[0:dh, :], inc=(env.gc, 16))
                bld.wait(T, env.gc)
            if getattr(cfg, "nc_dbg", None) is not None and h == 0 and b_ == 0:
                bld.wait(G, env.ev)
                bld.op(G, "dma_start", out=cfg.nc_dbg, in_=env.negcrep[:, 0:1], inc=(env.gc, 16))
                bld.op(G, "dma_start", out=cfg.vp_dbg, in_=env.VP[:, 0:2080], inc=(env.gc, 16))
                bld.op(G, "dma_start", out=cfg.sc_dbg, in_=env.sc11[:, :], inc=(env.gc, 16))
                bld.op(G, "dma_start", out=cfg.kn_dbg2, in_=cfg.kn2max[:, :], inc=(env.gc, 16))
                bld.wait(T, env.gc)
            bld.wait(T, env.ld2)
            bld.wait(T, env.gc)
            bld.wait(Sc, env.ev)
            exp_done = [0, 0]
            pv_done = [0, 0]
            for kt in range(NBLK):
                sp = kt % 2
                bld.wait(T, env.es, exp_done[sp])
                if cfg.skj is not None:
                    for j in range(4):
                        bld.op(T, "matmul", out=env.psS[sp][:, j * 128:(j + 1) * 128],
                               lhsT=cfg.id_bf,
                               rhs=env.SKJH[:, j * 4096 + kt * 128: j * 4096 + (kt + 1) * 128],
                               start=(j == 0), stop=False, skip_group_check=True)
                if dh == 64:
                    klhs = env.KBuf[0:64, kt * 128:(kt + 1) * 128]
                else:
                    klhs = env.KBuf[:, kt * 128:(kt + 1) * 128]
                bld.op(T, "matmul", out=env.psS[sp][:, :], lhsT=klhs, rhs=env.qbh[0:dh, :],
                       start=(cfg.skj is None), stop=True, inc=env.ts,
                       skip_group_check=True)
                t_mark = env.ts.n
                bld.wait(Sc, env.pv, pv_done[sp])
                bld.wait(Sc, env.ts, t_mark)
                bld.op(Sc, "activation", out=env.PT[sp][:, :], in_=env.psS[sp][:, :],
                       func=AF.Exp, bias=env.negcrep[:, 0:1], inc=env.es)
                exp_done[sp] = env.es.n
                if getattr(cfg, "pt_dbg", None) is not None and h == 0 and b_ == 0 and kt == 0:
                    bld.wait(G, env.es)
                    bld.op(G, "dma_start", out=cfg.pt_dbg, in_=env.PT[sp][:, :], inc=(env.gc, 16))
                    bld.wait(T, env.gc)
                bld.wait(T, env.es, exp_done[sp])
                bld.op(T, "matmul", out=env.psO[0:min(dh + 1, 128), :],
                       lhsT=env.VP[:, vp0 + kt * estride: vp0 + kt * estride + min(dh + 1, estride)],
                       rhs=env.PT[sp][:, :], start=(kt == 0), stop=(kt == NBLK - 1),
                       inc=env.pv, skip_group_check=True)
                if not cfg.l_ones:
                    bld.op(T, "matmul", out=env.psL[0:1, :], lhsT=cfg.ones_col_bf[0:128, 0:1],
                           rhs=env.PT[sp][:, :], start=(kt == 0), stop=(kt == NBLK - 1),
                           inc=env.pv, skip_group_check=True)
                pv_done[sp] = env.pv.n
            prev_es = env.es.n
            bld.wait(V, env.pv)
            lsrc = env.psO[dh:dh + 1, :] if cfg.l_ones else env.psL[0:1, :]
            bld.sop(V, "tensor_copy", out=env.lrow[0:1, :], in_=lsrc, inc=env.ev)
            if getattr(cfg, "l_dbg", None) is not None and h == 0:
                bld.wait(G, env.ev)
                bld.op(G, "dma_start", out=cfg.l_dbg[0:1, b0:b0 + 512],
                       in_=env.lrow[0:1, :], inc=(env.gc, 16))
                bld.wait(V, env.gc)
            bld.sop(V, "reciprocal", out=env.lrow[0:1, :], in_=env.lrow[0:1, :], inc=env.ev)
            bld.wait(T, env.ev)
            bld.op(T, "matmul", out=env.psAux[0:dh, :], lhsT=env.ones_row_f32[0:1, 0:dh],
                   rhs=env.lrow[0:1, :], start=True, stop=True, inc=env.mm)
            bld.wait(V, env.mm)
            bld.op(V, "tensor_copy", out=env.linv[0:dh, :], in_=env.psAux[0:dh, :])
            bld.op(V, "tensor_mul",
                   env.BIGA[hp:hp + dh, hc * LOCT + b0: hc * LOCT + b0 + 512],
                   env.psO[0:dh, :], env.linv[0:dh, :], inc=env.ev)
        ts_end[h] = env.ts.n
    bld.barrier()



def _alloc_common(nc, ctx, bld, sbf_cols=10752, sf32_cols=6656):
    E = ctx.enter_context
    env = Env()
    env.X = E(nc.sbuf_tensor("Xa", [128, NKC * LOCT], F32))
    env.XN = E(nc.sbuf_tensor("XNa", [128, NKC * LOCT], BF16))
    env.BIGA = E(nc.sbuf_tensor("BIGAa", [128, NKC * LOCT], BF16))
    env.SBF = E(nc.sbuf_tensor("SBFa", [128, sbf_cols], BF16))
    env.SF32 = E(nc.sbuf_tensor("SF32a", [128, sf32_cols], F32))
    env.rows = E(nc.sbuf_tensor("rows", [1, 3072], F32))
    env.sc11 = E(nc.sbuf_tensor("sc11", [1, 16], F32))
    env.negcrep = E(nc.sbuf_tensor("negcrep", [128, 2], F32))
    env.c_ones_col_f32 = E(nc.sbuf_tensor("c1cf", [128, 1], F32))
    env.c_ones_row_f32 = E(nc.sbuf_tensor("c1rf", [1, 128], F32))
    env.c_ones_col_bf = E(nc.sbuf_tensor("c1cb", [128, 1], BF16))
    env.c_id_bf = E(nc.sbuf_tensor("cidb", [128, 128], BF16))
    env.c_id_f32 = E(nc.sbuf_tensor("cidf", [128, 128], F32))
    env.c_onesh = E(nc.sbuf_tensor("c1h", [128, 2], BF16))
    env.ps = [E(nc.psum_tensor(f"ps{i}", [128, 512], F32)) for i in range(8)]
    for name in ("ld", "ld2", "mm", "ev", "sc", "gc", "gcast", "st", "ts", "es", "pv"):
        setattr(env, name, bld.sem(name))
    env.ones_col_f32 = env.c_ones_col_f32[:, 0:1]
    env.ones_row_f32 = env.c_ones_row_f32[0:1, :]
    env.ps_mean = env.ps[4]
    env.ps_sq = env.ps[5]
    env.ps_rep = env.ps[3]
    env.psAux = env.ps[3]
    env.psS = [env.ps[0], env.ps[1]]
    env.psO = env.ps[2]
    env.psL = env.ps[6]
    env.XG = env.SF32[:, 0:4096]
    env.rg_c = env.SF32[:, 4096:4608]
    env.rg_s = env.SF32[:, 4608:5120]
    env.kev = env.SF32[:, 5120:5632]
    env.kev2 = env.SF32[:, 5632:6144]
    env.stage2 = [env.SF32[:, 4096:5120], env.SF32[:, 5120:6144]]
    env.murep = env.SF32[:, 5120:5632]
    env.rstdrep = env.SF32[:, 5632:6144]
    env.vec_scr = env.SF32[:, 6144:6656]
    env.sq_scr = env.SF32[:, 6144:6656]
    env.qf = env.SF32[:, 0:512]
    env.qscr = env.SF32[:, 512:1024]
    env.wq_stage = env.SF32[:, 1024:2048]
    env.linv = env.SF32[:, 2048:2560]
    env.skj_stage = [env.SF32[:, 4096:5120], env.SF32[:, 5120:6144]]
    env.mu_g = env.rows[0:1, 0:512]
    env.rstd_g = env.rows[0:1, 512:1024]
    env.mu_p = env.rows[0:1, 1024:1536]
    env.rstd_p = env.rows[0:1, 1536:2048]
    env.lrow = env.rows[0:1, 2048:2560]
    env.row_tmp = env.rows[0:1, 2560:3072]
    env.row_tmp2 = env.rows[0:1, 2048:2560]
    return env


def _load_x_and_consts(bld, env, xo, const_pairs):
    bld.op(Y, "dma_start", out=env.X[:, :], in_=xo, inc=(env.ld, 16))
    for dst, src in const_pairs:
        bld.op(Y, "dma_start", out=dst, in_=src, inc=(env.ld, 16))
    bld.wait(Y, env.ld)
    bld.barrier()


def _owned_ln0(bld, env, rows_dram, write_xn=True):
    for g in range(4):
        c0 = g * 512
        emit_ln_group(
            bld, env,
            xsrc=lambda kc, c0=c0: env.X[:, kc * LOCT + c0: kc * LOCT + c0 + 512],
            nkc=NKC, ncols=512,
            mu_row=env.mu_g, rstd_row=env.rstd_g,
            xn_dst=(lambda kc, c0=c0: env.XN[:, kc * LOCT + c0: kc * LOCT + c0 + 512])
            if write_xn else None)
        bld.wait(G, env.ev)
        bld.wait(G, env.sc)
        bld.op(G, "dma_start", out=rows_dram[0:1, c0:c0 + 512], in_=env.mu_g,
               inc=(env.gc, 16))
        bld.op(G, "dma_start", out=rows_dram[1:2, c0:c0 + 512], in_=env.rstd_g,
               inc=(env.gc, 16))
        bld.wait(V, env.gc)
    bld.barrier()


def _residual_epilogue(bld, env, rows_dram, proj_emit, bias_arena, halfed=True,
                       extra=None):
    sc_f = 0.5 if halfed else 1.0
    for g in range(4):
        c0 = g * 512
        bld.wait(Y, env.ev)
        bld.op(Y, "dma_start", out=env.mu_g, in_=rows_dram[0:1, c0:c0 + 512],
               inc=(env.ld, 16))
        bld.op(Y, "dma_start", out=env.rstd_g, in_=rows_dram[1:2, c0:c0 + 512],
               inc=(env.ld, 16))
        bld.wait(V, env.ld)
        bld.op(V, "tensor_scalar_mul", env.rstd_g, env.rstd_g, sc_f, inc=env.ev)
        emit_replicate(bld, env, env.mu_g, env.murep[:, 0:512])
        emit_replicate(bld, env, env.rstd_g, env.rstdrep[:, 0:512])
        for oc in range(NKC):
            bank = env.ps[oc % 2]
            bld.wait(T, env.ev)
            proj_emit(bld, env, oc, g, bank)
            bld.wait(Sc, env.mm)
            bld.op(Sc, "activation", out=env.qf[:, :], in_=bank[:, :], func=AF.Identity,
                   bias=bias_arena[:, oc:oc + 1], scale=sc_f, inc=env.sc)
            xt = env.X[:, oc * LOCT + c0: oc * LOCT + c0 + 512]
            bld.wait(V, env.sc)
            bld.op(V, "tensor_sub", env.qscr[:, :], xt, env.murep[:, 0:512])
            bld.op(V, "tensor_mul", env.qscr[:, :], env.qscr[:, :], env.rstdrep[:, 0:512])
            bld.op(V, "tensor_add", env.qscr[:, :], env.qscr[:, :], env.qf[:, :])
            if extra is not None:
                extra(bld, env, oc, g)
            bld.op(V, "tensor_add", xt, xt, env.qscr[:, :], inc=env.ev)
    bld.barrier()


def build_ffn0(debug=False, ext_kv=False):
    nc = bass.Bass("TRN2", target_bir_lowering=False)
    names = {}
    def din(name, shape, dt=F32):
        names[name] = nc.dram_tensor(name, shape, dt, kind="ExternalInput").ap()
        return names[name]

    xo = din("xo", [128, NKC * LOCT])
    xg = wkT = wvT = kb = vb_row = rc_glob = rs_glob = None
    if not ext_kv:
        xg = din("xg", [128, NKC * GT])
        wkT = din("wkT", [128, NKC * D]); wvT = din("wvT", [128, NKC * D])
        kb = din("kb", [128, 8]); vb_row = din("vb_row", [1, D])
    wqT = din("wqT", [128, NKC * D])
    qb = din("qb", [128, 8])
    gwT = din("gwT", [128, NKC * 8]); gb_row = din("gb_row", [1, 8])
    ebias = din("ebias", [8, D])
    weT = din("weT", [128, 8 * NKC * D])
    ffnT = din("ffnT", [128, NKC * D]); ffnb = din("ffnb", [128, 8])
    rc_loc = din("rc_loc", [64, LOCT], BF16); rs_loc = din("rs_loc", [64, LOCT], BF16)
    if not ext_kv:
        rc_glob = din("rc_glob", [128, GT]); rs_glob = din("rs_glob", [128, GT])
    ones_col_f32 = din("ones_col_f32", [128, 1]); ones_row_f32 = din("ones_row_f32", [1, 128])
    ones_col_bf = din("ones_col_bf", [128, 1], BF16); onesh = din("onesh", [128, 2], BF16)
    id_bf = din("id_bf", [128, 128], BF16); id_f32 = din("id_f32", [128, 128])

    dk = dict(kind="ExternalOutput") if debug else {}
    if ext_kv:
        kdram = nc.dram_tensor("kdram", [128, NKC * GT], BF16, kind="ExternalInput").ap()
        v2 = nc.dram_tensor("v2dram", [16, NBLK * B, 128, 64], BF16, kind="ExternalInput").ap()
        kn8 = din("kn8", [1, 8])
    else:
        kdram = nc.dram_tensor("kdram", [128, NKC * GT], BF16, **dk).ap()
        v2 = nc.dram_tensor("v2dram", [16, NBLK * B, 128, 64], BF16, **dk).ap()
    wfm_dram = nc.dram_tensor("wfmd", [8, LOCT], F32, **dk).ap()
    rows_dram = nc.dram_tensor("rowsd", [2, LOCT], F32, **dk).ap()
    if debug:
        xn_dbg = nc.dram_tensor("xn_dbg", [128, NKC * LOCT], BF16, kind="ExternalOutput").ap()
        at_dbg = nc.dram_tensor("at_dbg", [128, NKC * LOCT], BF16, kind="ExternalOutput").ap()
        x4_dbg = nc.dram_tensor("x4_dbg", [128, NKC * LOCT], F32, kind="ExternalOutput").ap()
        x3_dbg = nc.dram_tensor("x3_dbg", [128, NKC * LOCT], F32, kind="ExternalOutput").ap()
        moe_dbg = nc.dram_tensor("moe_dbg", [128, NKC * LOCT], BF16, kind="ExternalOutput").ap()
        kn_dbg = nc.dram_tensor("kn_dbg", [1, 16], F32, kind="ExternalOutput").ap()
        wk_dbg = nc.dram_tensor("wk_dbg", [128, NKC * D], BF16, kind="ExternalOutput").ap()
        q_dbg = nc.dram_tensor("q_dbg", [64, LOCT], BF16, kind="ExternalOutput").ap()
        qf_dbg = nc.dram_tensor("qf_dbg", [64, LOCT], F32, kind="ExternalOutput").ap()
        nc_dbg = nc.dram_tensor("nc_dbg", [128, 1], F32, kind="ExternalOutput").ap()
        pt_dbg = nc.dram_tensor("pt_dbg", [128, 512], BF16, kind="ExternalOutput").ap()
        vp_dbg = nc.dram_tensor("vp_dbg", [128, 2080], BF16, kind="ExternalOutput").ap()
        sc_dbg = nc.dram_tensor("sc_dbg", [1, 16], F32, kind="ExternalOutput").ap()
        kn_dbg2 = nc.dram_tensor("kn_dbg2", [1, 1], F32, kind="ExternalOutput").ap()
        wq_dbg = nc.dram_tensor("wq_dbg", [128, 512], BF16, kind="ExternalOutput").ap()
        l_dbg = nc.dram_tensor("l_dbg", [1, LOCT], F32, kind="ExternalOutput").ap()
    xout = nc.dram_tensor("xout", [128, NKC * LOCT], F32, kind="ExternalOutput").ap()

    with ExitStack() as ctx:
        bld = Bld(nc, ctx)
        env = _alloc_common(nc, ctx, bld, sbf_cols=9728)
        E = ctx.enter_context
        t_qb = E(nc.sbuf_tensor("tqb", [128, 8], F32))
        t_kb = E(nc.sbuf_tensor("tkb", [128, 8], F32))
        t_vb = E(nc.sbuf_tensor("tvb", [1, D], F32))
        t_gb = E(nc.sbuf_tensor("tgb", [1, 8], F32))
        t_eb = E(nc.sbuf_tensor("teb", [8, D], F32))
        t_ffnb = E(nc.sbuf_tensor("tffnb", [128, 8], F32))
        t_gw = E(nc.sbuf_tensor("tgw", [128, NKC * 8], BF16))
        rloc_c = E(nc.sbuf_tensor("rlc", [64, LOCT], BF16))
        rloc_s = E(nc.sbuf_tensor("rls", [64, LOCT], BF16))
        wtp = E(nc.sbuf_tensor("wtp", [8, 128], F32))
        wfm8g = E(nc.sbuf_tensor("wfm8g", [8, 512], F32))
        wrow = E(nc.sbuf_tensor("wrow", [1, 512], F32))
        kn2max = E(nc.sbuf_tensor("kn2m", [1, 1], F32))
        gw_stage = E(nc.sbuf_tensor("gwst", [128, NKC * 8], F32))

        const_pairs = [
            (env.c_ones_col_f32[:, :], ones_col_f32), (env.c_ones_row_f32[:, :], ones_row_f32),
            (env.c_ones_col_bf[:, :], ones_col_bf), (env.c_onesh[:, :], onesh),
            (env.c_id_bf[:, :], id_bf), (env.c_id_f32[:, :], id_f32),
            (t_qb[:, :], qb), (t_gb[:, :], gb_row),
            (t_eb[:, :], ebias), (t_ffnb[:, :], ffnb),
            (rloc_c[:, :], rc_loc), (rloc_s[:, :], rs_loc), (gw_stage[:, :], gwT)]
        if not ext_kv:
            const_pairs += [(t_kb[:, :], kb), (t_vb[:, :], vb_row)]
        _load_x_and_consts(bld, env, xo, const_pairs)
        bld.op(V, "tensor_scalar_mul", t_qb[:, :], t_qb[:, :], 0.125)
        bld.op(V, "tensor_scalar_mul", t_ffnb[:, :], t_ffnb[:, :], 0.5)
        bld.op(V, "memset", kn2max[:, :], 0.0)
        bld.op(V, "tensor_copy", out=t_gw[:, :], in_=gw_stage[:, :], inc=env.ev)
        if ext_kv:
            t_kn8 = E(nc.sbuf_tensor("tkn8", [1, 8], F32))
            bld.op(Y, "dma_start", out=t_kn8[:, :], in_=kn8, inc=(env.ld, 16))
            bld.wait(V, env.ld)
            bld.sop(V, "tensor_reduce", kn2max[:, :], t_kn8[:, :], AX.X, ALU.max,
                    inc=env.ev)
        else:
            WKB = env.BIGA[:, 4096:12288]
            WVB = env.SBF[:, 0:8192]
            emit_load_cast(bld, env, lambda c0, c1: wkT[:, c0:c1], env.stage2, WKB, NKC * D, 1024)
            emit_load_cast(bld, env, lambda c0, c1: wvT[:, c0:c1], env.stage2, WVB, NKC * D, 1024)
        bld.barrier()

        _owned_ln0(bld, env, rows_dram)

        if debug:
            bld.op(G, "dma_start", out=xn_dbg, in_=env.XN[:, :], inc=(env.gc, 16))
            bld.op(G, "dma_start", out=wk_dbg, in_=env.BIGA[:, 4096:12288], inc=(env.gc, 16))
            bld.wait(G, env.gc)
        env.xng = env.BIGA[:, 0:4096]
        cfg = Env()
        cfg.dh = 64
        cfg.n_tg = GT // 512
        cfg.xsrc_dram = xg
        cfg.rope = True
        cfg.rc_glob = rc_glob
        cfg.rs_glob = rs_glob
        cfg.WKB = cfg.WVB = None
        if not ext_kv:
            cfg.WKB, cfg.WVB = WKB, WVB
        cfg.t_kb, cfg.t_vb = t_kb, t_vb
        cfg.kdram, cfg.v2 = kdram, v2
        cfg.kd_t = GT
        cfg.kn2max = kn2max
        cfg.onesh = env.c_onesh
        cfg.ones_col_bf = env.c_ones_col_bf[:, 0:1]
        cfg.kbf = env.SBF[:, 8192:8704]
        cfg.vbf = env.SBF[:, 9216:9728]
        env.sqb = env.SBF[:, 8704:9216]
        cfg.kstore, cfg.vstore = {}, {}
        def xn_maker(bld_, env_, tg):
            emit_ln_group(bld_, env_,
                          xsrc=lambda kc: env_.XG[:, kc * 512:(kc + 1) * 512],
                          nkc=NKC, ncols=512, mu_row=env_.mu_p, rstd_row=env_.rstd_p,
                          xn_dst=lambda kc: env_.xng[:, kc * 512:(kc + 1) * 512])
        cfg.xn_maker = xn_maker
        if not ext_kv:
            emit_kv_prologue(bld, env, cfg)

        acfg = Env()
        acfg.heads, acfg.dh = 16, 64
        acfg.scale = 0.125
        acfg.rope = True
        acfg.rloc_c, acfg.rloc_s = rloc_c, rloc_s
        acfg.skj = None
        acfg.bmax_row = None
        acfg.l_ones = True
        acfg.vp_bufs = 1
        acfg.kdram, acfg.v2 = kdram, v2
        acfg.kd_t = GT
        acfg.wqT = wqT
        acfg.t_qb = t_qb
        acfg.kn2max = kn2max
        acfg.ones_col_bf = env.c_ones_col_bf
        acfg.id_bf = env.c_id_bf[:, :]
        if debug:
            acfg.q_dbg, acfg.l_dbg = q_dbg, l_dbg
            acfg.qf_dbg, acfg.wq_dbg = qf_dbg, wq_dbg
            acfg.nc_dbg, acfg.pt_dbg, acfg.vp_dbg = nc_dbg, pt_dbg, vp_dbg
            acfg.sc_dbg, acfg.kn_dbg2 = sc_dbg, kn_dbg2
        env.VP = env.SBF[:, 0:2080]
        env.KBuf = env.SBF[:, 2080:6176]
        env.PT = [env.SBF[:, 6176 + i * 512:6176 + (i + 1) * 512] for i in range(2)]
        env.WQ = [env.SBF[:, 7200 + i * 512:7200 + (i + 1) * 512] for i in range(2)]
        env.qbh = env.SBF[:, 8224:8736]
        env.sqq = env.SBF[:, 8736:9248]
        bld.op(V, "memset",
               env.VP.rearrange("p (k e) -> p k e", e=65)[:, :, 64:65],
               1.0, inc=env.ev)
        bld.barrier()
        emit_attention(bld, env, acfg)
        if debug:
            bld.op(G, "dma_start", out=at_dbg, in_=env.BIGA[:, :], inc=(env.gc, 16))
            bld.op(G, "dma_start", out=kn_dbg, in_=env.sc11[:, :], inc=(env.gc, 16))
            bld.barrier()

        gsc = env.SF32[:, 0:16]
        w_tm = env.SF32[:, 16:32]
        for tile in range(16):
            t0 = tile * 128
            bank = env.ps[tile % 2]
            bld.wait(T, env.ev)
            bld.op(T, "matmul", out=bank[:, 0:8], lhsT=env.ones_row_f32,
                   rhs=t_gb[0:1, :], start=True, stop=False, skip_group_check=True)
            for kc in range(NKC):
                bld.op(T, "matmul", out=bank[:, 0:8],
                       lhsT=env.BIGA[:, kc * LOCT + t0: kc * LOCT + t0 + 128],
                       rhs=t_gw[:, kc * 8:(kc + 1) * 8],
                       start=False, stop=(kc == NKC - 1), skip_group_check=True,
                       inc=(env.mm if kc == NKC - 1 else ()))
            bld.wait(V, env.mm)
            bld.sop(V, "tensor_copy", out=gsc[:, 0:8], in_=bank[:, 0:8])
            bld.sop(V, "reduce_max", gsc[:, 8:9], gsc[:, 0:8], axis=AX.X)
            bld.sop(V, "tensor_scalar_mul", gsc[:, 9:10], gsc[:, 8:9], -1.0, inc=env.ev)
            bld.wait(Sc, env.ev)
            bld.sop(Sc, "activation", out=gsc[:, 0:8], in_=gsc[:, 0:8], func=AF.Exp,
                    bias=gsc[:, 9:10], inc=env.sc)
            bld.wait(V, env.sc)
            bld.sop(V, "reduce_sum", gsc[:, 8:9], gsc[:, 0:8], axis=AX.X)
            bld.sop(V, "reciprocal", gsc[:, 9:10], gsc[:, 8:9])
            bld.sop(V, "tensor_scalar", gsc[:, 0:8], gsc[:, 0:8], gsc[:, 9:10], None,
                    ALU.mult)
            bld.sop(V, "reduce_max", gsc[:, 8:9], gsc[:, 0:8], axis=AX.X)
            bld.sop(V, "tensor_scalar", w_tm[:, 0:8], gsc[:, 0:8], gsc[:, 8:9], None,
                    ALU.is_ge)
            bld.sop(V, "scalar_tensor_tensor", w_tm[:, 0:8], w_tm[:, 0:8], -1e30,
                    gsc[:, 0:8], ALU.mult, ALU.add)
            bld.sop(V, "reduce_max", gsc[:, 9:10], w_tm[:, 0:8], axis=AX.X)
            bld.sop(V, "tensor_scalar", w_tm[:, 0:8], gsc[:, 0:8], gsc[:, 9:10], None,
                    ALU.is_ge)
            bld.sop(V, "tensor_mul", w_tm[:, 0:8], gsc[:, 0:8], w_tm[:, 0:8], inc=env.ev)
            bld.wait(T, env.ev)
            bld.op(T, "transpose", out=env.ps[2][0:8, 0:128], in_=w_tm[:, 0:8],
                   identity=env.c_id_f32[:, :], inc=env.mm)
            bld.wait(V, env.mm)
            bld.wait(V, env.gc)
            bld.op(V, "tensor_copy", out=wtp[0:8, :], in_=env.ps[2][0:8, 0:128],
                   inc=env.ev)
            bld.wait(G, env.ev)
            bld.op(G, "dma_start", out=wfm_dram[0:8, t0:t0 + 128], in_=wtp[0:8, :],
                   inc=(env.gc, 16))
        bld.barrier()

        WEB = [env.SBF[:, 4096 + i * 1024:4096 + (i + 1) * 1024] for i in range(2)]
        XeG = env.SBF[:, 0:4096]
        wrepG = env.SF32[:, 0:4096]
        web_done = {}
        web_mmuse = {}
        for g in range(4):
            c0 = g * 512
            bld.wait(Y, env.ev)
            bld.wait(Y, env.gc)
            bld.op(Y, "dma_start", out=wfm8g[0:8, :], in_=wfm_dram[0:8, c0:c0 + 512],
                   inc=(env.ld, 16))
            for e in range(8):
                bld.wait(Y, env.ev)
                bld.op(Y, "dma_start", out=wrow[0:1, :],
                       in_=wfm_dram[e:e + 1, c0:c0 + 512], inc=(env.ld, 16))
                bld.wait(V, env.ld)
                bld.wait(T, env.ld)
                emit_replicate(bld, env, wrow[0:1, :], wrepG[:, e * 512:(e + 1) * 512])
            for e in range(8):
                bld.wait(V, env.mm)
                for kc in range(NKC):
                    bld.op(V, "tensor_mul", XeG[:, kc * 512:(kc + 1) * 512],
                           env.BIGA[:, kc * LOCT + c0: kc * LOCT + c0 + 512],
                           wrepG[:, e * 512:(e + 1) * 512], inc=env.ev)
                for oc in range(NKC):
                    par = oc % 2
                    bld.wait(Y, env.gcast, web_done.get(par, 0))
                    bld.op(Y, "dma_start",
                           out=env.stage2[par][:, 0:1024].rearrange("p (k c) -> p k c", c=128),
                           in_=weT.rearrange("p (e k c) -> p e k c", e=8, c=D)
                               [:, e:e + 1, :, oc * 128:(oc + 1) * 128].squeeze(1),
                           inc=(env.ld, 16))
                    bld.wait(G, env.ld)
                    bld.wait(G, env.mm, web_mmuse.get(par, 0))
                    bld.op(G, "tensor_copy", out=WEB[par][:, :],
                           in_=env.stage2[par][:, 0:1024], inc=env.gcast)
                    web_done[par] = env.gcast.n
                    bank = env.ps[oc]
                    bld.wait(T, env.gcast)
                    bld.wait(T, env.ev)
                    if e == 0:
                        bld.op(T, "matmul", out=bank[:, :],
                               lhsT=t_eb[0:8, oc * 128:(oc + 1) * 128],
                               rhs=wfm8g[0:8, :], start=True, stop=False,
                               skip_group_check=True)
                    for kc in range(NKC):
                        bld.op(T, "matmul", out=bank[:, :],
                               lhsT=WEB[par][:, kc * 128:(kc + 1) * 128],
                               rhs=XeG[:, kc * 512:(kc + 1) * 512],
                               start=False, stop=(e == 7 and kc == NKC - 1),
                               skip_group_check=True, inc=env.mm)
                    web_mmuse[par] = env.mm.n
            bld.wait(V, env.mm)
            for oc in range(NKC):
                bld.op(V, "tensor_copy",
                       out=env.BIGA[:, oc * LOCT + c0: oc * LOCT + c0 + 512],
                       in_=env.ps[oc][:, :], inc=env.ev)
        bld.barrier()
        if debug:
            bld.op(G, "dma_start", out=moe_dbg, in_=env.BIGA[:, :], inc=(env.gc, 16))
            bld.barrier()

        FFNB = env.SBF[:, 0:8192]
        emit_load_cast(bld, env, lambda c0, c1: ffnT[:, c0:c1], env.stage2, FFNB,
                       NKC * D, 1024)
        bld.barrier()

        def proj_emit(bld_, env_, oc, g, bank):
            c0 = g * 512
            for kc in range(NKC):
                bld_.op(T, "matmul", out=bank[:, :],
                        lhsT=FFNB[:, kc * D + oc * 128: kc * D + (oc + 1) * 128],
                        rhs=env_.BIGA[:, kc * LOCT + c0: kc * LOCT + c0 + 512],
                        start=(kc == 0), stop=(kc == NKC - 1),
                        inc=(env_.mm if kc == NKC - 1 else ()))
        _residual_epilogue(bld, env, rows_dram, proj_emit, t_ffnb, halfed=True)

        bld.op(G, "dma_start", out=xout, in_=env.X[:, :], inc=(env.st, 16))
        bld.wait(G, env.st)
        bld.finalize()
    return nc


def host_shared_ffn0(inp):
    d = {}
    d["wqT"] = prep_wT(inp["L0_ffn_q_w"])
    d["wkT"] = prep_wT(inp["L0_ffn_k_w"])
    d["wvT"] = prep_wT(inp["L0_ffn_v_w"])
    d["qb"] = prep_bias(inp["L0_ffn_q_b"])
    d["kb"] = prep_bias(inp["L0_ffn_k_b"])
    d["vb_row"] = np.ascontiguousarray(inp["L0_ffn_v_b"].reshape(1, D), np.float32)
    d["gwT"] = prep_wT(inp["L0_ffn_gate_w"])
    d["gb_row"] = np.ascontiguousarray(inp["L0_ffn_gate_b"].reshape(1, 8), np.float32)
    d["ebias"] = np.ascontiguousarray(inp["L0_ffn_expert_b"], np.float32)
    d["weT"] = np.concatenate([prep_wT(inp["L0_ffn_expert_w"][e]) for e in range(8)], axis=1)
    d["ffnT"] = prep_wT(inp["L0_ffn_ffn_w"])
    d["ffnb"] = prep_bias(inp["L0_ffn_ffn_b"])
    cg, sg = rope_tables(global_positions())
    d["rc_glob"], d["rs_glob"] = cg, sg
    d["ones_col_f32"] = np.ones((128, 1), np.float32)
    d["ones_row_f32"] = np.ones((1, 128), np.float32)
    d["ones_col_bf"] = np.ones((128, 1), bf)
    h2 = np.zeros((128, 2), bf)
    h2[0:64, 0] = 1
    h2[64:128, 1] = 1
    d["onesh"] = h2
    d["id_bf"] = np.eye(128, dtype=bf)
    d["id_f32"] = np.eye(128, dtype=np.float32)
    return d


def host_core_ffn0(xfull, xg, shared, c):
    d = dict(shared)
    d["xo"] = shard_x(xfull, c)
    d["xg"] = xg
    cl, sl_ = rope_tables(local_positions(c))
    d["rc_loc"] = cl[0:64].astype(bf)
    d["rs_loc"] = sl_[0:64].astype(bf)
    return d


def run_phase_ffn0(nc, xfull, shared):
    xg = x_global(xfull)
    in_maps = [host_core_ffn0(xfull, xg, shared, c) for c in range(NCORE)]
    res = run_bass_kernel_spmd(nc, in_maps, core_ids=list(range(NCORE)))
    return unshard_x([r["xout"] for r in res.results])



def emit_attention_skj(bld, env, cfg):
    dh = 64
    estride = 65
    qmm_end = {}
    ts_end = {}
    prev_es = 0
    bld.wait(Y, env.gc)
    bld.wait(G, env.gc)
    psO_b = [env.ps[2], env.ps[4], env.ps[5], env.ps[6]]
    for h in range(16):
        hp = (h % 2) * 64
        hc = h // 2
        par_h = h % 2
        bld.wait(Y, env.gcast)
        bld.op(Y, "dma_start",
               out=env.wq_stage[:, 0:NKC * dh].rearrange("p (k c) -> p k c", c=dh),
               in_=cfg.wqT.rearrange("p (k c) -> p k c", c=D)[:, :, h * dh:(h + 1) * dh],
               inc=(env.ld, 16))
        bld.wait(G, env.ld)
        bld.wait(G, env.mm, qmm_end.get(h - 2, 0))
        bld.op(G, "tensor_copy", out=env.WQ[par_h][:, 0:NKC * dh],
               in_=env.wq_stage[:, 0:NKC * dh], inc=env.gcast)
        for b_ in range(4):
            b0 = b_ * 512
            bld.wait(T, env.gcast)
            bld.wait(T, env.ev)
            for kc in range(NKC):
                bld.op(T, "matmul", out=env.psAux[0:dh, :],
                       lhsT=env.WQ[par_h][:, kc * dh:(kc + 1) * dh],
                       rhs=env.XN[:, kc * LOCT + b0: kc * LOCT + b0 + 512],
                       start=(kc == 0), stop=(kc == NKC - 1),
                       inc=(env.mm if kc == NKC - 1 else ()))
            if b_ == 3:
                qmm_end[h] = env.mm.n
            bld.wait(Sc, env.mm)
            bld.op(Sc, "activation", out=env.qbh4[0:dh, b0:b0 + 512],
                   in_=env.psAux[0:dh, :], func=AF.Identity,
                   bias=cfg.t_qb[hp:hp + dh, hc:hc + 1], scale=cfg.scale, inc=env.sc)
            bld.op(Sc, "activation", out=env.sqq[0:dh, :],
                   in_=env.qbh4[0:dh, b0:b0 + 512], func=AF.Square, inc=env.sc)
            bld.wait(T, env.sc)
            bld.op(T, "matmul", out=env.psAux[0:1, :], lhsT=cfg.ones_col_bf[0:dh, :],
                   rhs=env.sqq[0:dh, :], start=True, stop=True, inc=env.mm)
            bld.wait(V, env.mm)
            bld.sop(V, "reduce_max", env.sc11[0:1, 1:2], env.psAux[0:1, :], axis=AX.X)
            bld.sop(V, "tensor_mul", env.sc11[0:1, 2:3], env.sc11[0:1, 1:2],
                    cfg.kn2max[0:1, 0:1], inc=env.ev)
            bld.wait(Sc, env.ev)
            bld.sop(Sc, "activation", out=env.sc11[0:1, 3:4], in_=env.sc11[0:1, 2:3],
                    func=AF.Sqrt, inc=env.sc)
            bld.wait(V, env.sc)
            bld.sop(V, "tensor_add", env.sc11[0:1, 3:4], env.sc11[0:1, 3:4],
                    cfg.bmax_row[0:1, h:h + 1])
            bld.sop(V, "tensor_scalar_mul", env.sc11[0:1, 4:5], env.sc11[0:1, 3:4], -1.0)
            bld.sop(V, "tensor_copy", env.sc11[0:1, 5:6], env.sc11[0:1, 4:5], inc=env.ev)
            bld.wait(T, env.ev)
            bld.op(T, "matmul", out=env.psAux[:, 0:2], lhsT=env.ones_row_f32,
                   rhs=env.sc11[0:1, 4:6], start=True, stop=True, inc=env.mm)
            bld.wait(V, env.mm)
            bld.wait(V, env.es, prev_es)
            bld.op(V, "tensor_copy", out=env.negc4[:, 2 * b_:2 * b_ + 2],
                   in_=env.psAux[:, 0:2], inc=env.ev)
        bld.wait(Sc, env.ev)
        exp_done = [0, 0]
        pv_done = [0, 0]
        for qi in range(4):
            kt0 = qi * 8
            skj_done = []
            for j in range(4):
                c0 = kt0 * 128
                par = j % 2
                if j >= 2:
                    bld.wait(Y, env.gcast, skj_done[j - 2])
                if j == 0:
                    bld.wait(G, env.ts, ts_end.get((h, qi - 1), ts_end.get((h - 1, 3), 0)))
                bld.op(Y, "dma_start", out=env.skj_stage[par][:, :],
                       in_=cfg.skj[h:h + 1, j:j + 1, :, c0:c0 + 1024].squeeze(1).squeeze(0),
                       inc=(env.ld, 16))
                bld.wait(G, env.ld)
                bld.op(G, "tensor_copy",
                       out=env.SKJH[:, j * 1024:(j + 1) * 1024],
                       in_=env.skj_stage[par][:, :], inc=env.gcast)
                skj_done.append(env.gcast.n)
            for b_ in range(4):
                b0 = b_ * 512
                bld.wait(Y, env.ts)
                bld.op(Y, "dma_start", out=env.KBuf[0:64, 0:1024],
                       in_=cfg.kdram[hp:hp + 64,
                                     hc * cfg.kd_t + b_ * S + kt0 * 128:
                                     hc * cfg.kd_t + b_ * S + (kt0 + 8) * 128],
                       inc=(env.ld2, 16))
                bld.wait(G, env.pv)
                blk0 = b_ * NBLK + kt0
                src = cfg.v2[h:h + 1, blk0:blk0 + 8, :, :].squeeze(0).transpose([1, 0, 2])
                dst = env.VP[:, 0:8 * estride] \
                    .rearrange("p (k e) -> p k e", e=estride)[:, :, 0:dh]
                bld.op(G, "dma_start", out=dst, in_=src, inc=(env.gc, 16))
                bld.wait(T, env.ld2)
                bld.wait(T, env.gc)
                bld.wait(T, env.gcast)
                bld.wait(Sc, env.ev)
                for kti in range(8):
                    kt = kt0 + kti
                    sp = kti % 2
                    bld.wait(T, env.es, exp_done[sp])
                    for j in range(4):
                        bld.op(T, "matmul", out=env.psS[sp][:, j * 128:(j + 1) * 128],
                               lhsT=cfg.id_bf,
                               rhs=env.SKJH[:, j * 1024 + kti * 128: j * 1024 + (kti + 1) * 128],
                               start=(j == 0), stop=False, skip_group_check=True)
                    bld.op(T, "matmul", out=env.psS[sp][:, :],
                           lhsT=env.KBuf[0:64, kti * 128:(kti + 1) * 128],
                           rhs=env.qbh4[0:dh, b0:b0 + 512],
                           start=False, stop=True, inc=env.ts, skip_group_check=True)
                    t_mark = env.ts.n
                    bld.wait(Sc, env.pv, pv_done[sp])
                    bld.wait(Sc, env.ts, t_mark)
                    bld.op(Sc, "activation", out=env.PT[sp][:, :], in_=env.psS[sp][:, :],
                           func=AF.Exp, bias=env.negc4[:, 2 * b_:2 * b_ + 1], inc=env.es)
                    exp_done[sp] = env.es.n
                    bld.wait(T, env.es, exp_done[sp])
                    bld.op(T, "matmul", out=psO_b[b_][0:dh + 1, :],
                           lhsT=env.VP[:, kti * estride: kti * estride + dh + 1],
                           rhs=env.PT[sp][:, :],
                           start=(kt == 0), stop=(kt == NBLK - 1),
                           inc=env.pv, skip_group_check=True)
                    pv_done[sp] = env.pv.n
                ts_end[(h, qi)] = env.ts.n
            prev_es = env.es.n
        for b_ in range(4):
            b0 = b_ * 512
            bld.wait(V, env.pv)
            bld.sop(V, "tensor_copy", out=env.lrow[0:1, :], in_=psO_b[b_][dh:dh + 1, :])
            bld.sop(V, "reciprocal", out=env.lrow[0:1, :], in_=env.lrow[0:1, :], inc=env.ev)
            bld.wait(T, env.ev)
            bld.op(T, "matmul", out=env.psAux[0:dh, :], lhsT=env.ones_row_f32[0:1, 0:dh],
                   rhs=env.lrow[0:1, :], start=True, stop=True, inc=env.mm)
            bld.wait(V, env.mm)
            bld.op(V, "tensor_copy", out=env.linv[0:dh, :], in_=env.psAux[0:dh, :])
            bld.op(V, "tensor_mul",
                   env.BIGA[hp:hp + dh, hc * LOCT + b0: hc * LOCT + b0 + 512],
                   psO_b[b_][0:dh, :], env.linv[0:dh, :], inc=env.ev)
    bld.barrier()


def build_attn0(ext_kv=False):
    nc = bass.Bass("TRN2", target_bir_lowering=False)
    def din(name, shape, dt=F32):
        return nc.dram_tensor(name, shape, dt, kind="ExternalInput").ap()

    xo = din("xo", [128, NKC * LOCT])
    xg = wkT = wvT = kb = vb_row = None
    if not ext_kv:
        xg = din("xg", [128, NKC * GT])
        wkT = din("wkT", [128, NKC * D]); wvT = din("wvT", [128, NKC * D])
        kb = din("kb", [128, 8]); vb_row = din("vb_row", [1, D])
    wqT = din("wqT", [128, NKC * D])
    qb = din("qb", [128, 8])
    projT = din("projT", [128, NKC * D]); projb = din("projb", [128, 8])
    lng = din("lng", [128, 8]); lnb = din("lnb", [128, 8])
    skj = din("skj", [16, 4, 128, NBLK * 128])
    rpbm = din("rpbm", [128, 16 * 32])
    ones_col_f32 = din("ones_col_f32", [128, 1]); ones_row_f32 = din("ones_row_f32", [1, 128])
    ones_col_bf = din("ones_col_bf", [128, 1], BF16); onesh = din("onesh", [128, 2], BF16)
    id_bf = din("id_bf", [128, 128], BF16); id_f32 = din("id_f32", [128, 128])

    if ext_kv:
        kdram = nc.dram_tensor("kdram", [128, NKC * GT], BF16, kind="ExternalInput").ap()
        v2 = nc.dram_tensor("v2dram", [16, NBLK * B, 128, 64], BF16, kind="ExternalInput").ap()
        kn8 = din("kn8", [1, 8])
    else:
        kdram = nc.dram_tensor("kdram", [128, NKC * GT], BF16).ap()
        v2 = nc.dram_tensor("v2dram", [16, NBLK * B, 128, 64], BF16).ap()
    rows_dram = nc.dram_tensor("rowsd", [2, LOCT], F32).ap()
    xout = nc.dram_tensor("xout", [128, NKC * LOCT], F32, kind="ExternalOutput").ap()

    with ExitStack() as ctx:
        bld = Bld(nc, ctx)
        env = _alloc_common(nc, ctx, bld, sbf_cols=11808)
        E = ctx.enter_context
        t_qb = E(nc.sbuf_tensor("tqb", [128, 8], F32))
        t_kb = E(nc.sbuf_tensor("tkb", [128, 8], F32))
        t_vb = E(nc.sbuf_tensor("tvb", [1, D], F32))
        t_projb = E(nc.sbuf_tensor("tprojb", [128, 8], F32))
        t_lng = E(nc.sbuf_tensor("tlng", [128, 8], F32))
        t_lnb = E(nc.sbuf_tensor("tlnb", [128, 8], F32))
        t_rpbm = E(nc.sbuf_tensor("trpbm", [128, 512], F32))
        bmax_row = E(nc.sbuf_tensor("bmaxr", [1, 16], F32))
        bm16 = E(nc.sbuf_tensor("bm16", [16, 128], F32))
        kn2max = E(nc.sbuf_tensor("kn2m", [1, 1], F32))

        _load_x_and_consts(bld, env, xo, [
            (env.c_ones_col_f32[:, :], ones_col_f32), (env.c_ones_row_f32[:, :], ones_row_f32),
            (env.c_ones_col_bf[:, :], ones_col_bf), (env.c_onesh[:, :], onesh),
            (env.c_id_bf[:, :], id_bf), (env.c_id_f32[:, :], id_f32),
            (t_qb[:, :], qb),
            (t_projb[:, :], projb), (t_lng[:, :], lng), (t_lnb[:, :], lnb),
            (t_rpbm[:, :], rpbm)] +
            ([] if ext_kv else [(t_kb[:, :], kb), (t_vb[:, :], vb_row)]))
        bld.op(V, "tensor_scalar_mul", t_qb[:, :], t_qb[:, :], 1.0 / 32.0)
        bld.op(V, "memset", kn2max[:, :], 0.0, inc=env.ev)
        bld.sop(V, "tensor_reduce", env.vec_scr[:, 0:16],
                t_rpbm.ap().rearrange("p (h u) -> p h u", h=16), AX.X, ALU.max, inc=env.ev)
        bld.wait(T, env.ev)
        bld.op(T, "transpose", out=env.ps[0][0:16, 0:128], in_=env.vec_scr[:, 0:16],
               identity=env.c_id_f32[:, :], inc=env.mm)
        bld.wait(V, env.mm)
        bld.sop(V, "tensor_copy", out=bm16[:, :], in_=env.ps[0][0:16, 0:128])
        bld.sop(V, "tensor_reduce", env.qf[0:16, 0:1], bm16[:, :], AX.X, ALU.max, inc=env.ev)
        bld.wait(T, env.ev)
        bld.op(T, "transpose", out=env.ps[0][0:1, 0:16], in_=env.qf[0:16, 0:1],
               identity=env.c_id_f32[0:16, 0:16], inc=env.mm)
        bld.wait(V, env.mm)
        bld.sop(V, "tensor_copy", out=bmax_row[:, :], in_=env.ps[0][0:1, 0:16], inc=env.ev)
        WKB = WVB = None
        if ext_kv:
            t_kn8 = E(nc.sbuf_tensor("tkn8", [1, 8], F32))
            bld.op(Y, "dma_start", out=t_kn8[:, :], in_=kn8, inc=(env.ld, 16))
            bld.wait(V, env.ld)
            bld.sop(V, "tensor_reduce", kn2max[:, :], t_kn8[:, :], AX.X, ALU.max,
                    inc=env.ev)
        else:
            WKB = env.BIGA[:, 4096:12288]
            WVB = env.SBF[:, 0:8192]
            emit_load_cast(bld, env, lambda c0, c1: wkT[:, c0:c1], env.stage2, WKB, NKC * D, 1024)
            emit_load_cast(bld, env, lambda c0, c1: wvT[:, c0:c1], env.stage2, WVB, NKC * D, 1024)
        bld.barrier()

        _owned_ln0(bld, env, rows_dram)

        env.xng = env.BIGA[:, 0:4096]
        cfg = Env()
        cfg.dh = 64
        cfg.n_tg = GT // 512
        cfg.xsrc_dram = xg
        cfg.rope = False
        cfg.WKB, cfg.WVB = WKB, WVB
        cfg.t_kb, cfg.t_vb = t_kb, t_vb
        cfg.kdram, cfg.v2 = kdram, v2
        cfg.kd_t = GT
        cfg.kn2max = kn2max
        cfg.onesh = env.c_onesh
        cfg.ones_col_bf = env.c_ones_col_bf[:, 0:1]
        cfg.kbf = env.SBF[:, 8192:8704]
        cfg.vbf = env.SBF[:, 9216:9728]
        env.sqb = env.SBF[:, 8704:9216]
        cfg.kstore, cfg.vstore = {}, {}

        def xn_maker(bld_, env_, tg):
            emit_ln_group(bld_, env_,
                          xsrc=lambda kc: env_.XG[:, kc * 512:(kc + 1) * 512],
                          nkc=NKC, ncols=512, mu_row=env_.mu_p, rstd_row=env_.rstd_p,
                          xn_dst=lambda kc: env_.xng[:, kc * 512:(kc + 1) * 512])
        cfg.xn_maker = xn_maker
        if not ext_kv:
            emit_kv_prologue(bld, env, cfg)

        acfg = Env()
        acfg.scale = 1.0 / 32.0
        acfg.skj = skj
        acfg.bmax_row = bmax_row[0:1, :]
        acfg.kdram, acfg.v2 = kdram, v2
        acfg.kd_t = GT
        acfg.wqT = wqT
        acfg.t_qb = t_qb
        acfg.kn2max = kn2max
        acfg.ones_col_bf = env.c_ones_col_bf
        acfg.id_bf = env.c_id_bf[:, :]
        env.SKJH = env.SBF[:, 0:4096]
        env.VP = env.SBF[:, 4096:6176]
        env.KBuf = env.SBF[:, 6176:7200]
        env.PT = [env.SBF[:, 7200 + i * 512:7200 + (i + 1) * 512] for i in range(2)]
        env.WQ = [env.SBF[:, 8224 + i * 512:8224 + (i + 1) * 512] for i in range(2)]
        env.qbh4 = env.SBF[:, 9248:11296]
        env.sqq = env.SBF[:, 11296:11808]
        env.negc4 = env.negcrep
        negc4 = E(nc.sbuf_tensor("negc4", [128, 8], F32))
        env.negc4 = negc4
        bld.op(V, "memset",
               env.VP.rearrange("p (k e) -> p k e", e=65)[:, :, 64:65], 1.0, inc=env.ev)
        bld.barrier()
        emit_attention_skj(bld, env, acfg)

        muY = env.rows[0:1, 1024:1536]
        rstdY = env.rows[0:1, 1536:2048]
        PJB = env.SBF[:, 0:8192]
        emit_load_cast(bld, env, lambda c0, c1: projT[:, c0:c1], env.stage2, PJB, NKC * D, 1024)
        bld.barrier()
        yscr = env.SF32[:, 0:512]
        for g in range(4):
            c0 = g * 512
            bld.wait(Y, env.ev)
            bld.op(Y, "dma_start", out=env.mu_g, in_=rows_dram[0:1, c0:c0 + 512], inc=(env.ld, 16))
            bld.op(Y, "dma_start", out=env.rstd_g, in_=rows_dram[1:2, c0:c0 + 512], inc=(env.ld, 16))
            bld.wait(V, env.ld)
            emit_replicate(bld, env, env.mu_g, env.murep[:, 0:512])
            emit_replicate(bld, env, env.rstd_g, env.rstdrep[:, 0:512])
            for pss in range(2):
                if pss == 1:
                    bld.wait(V, env.mm)
                    bld.sop(V, "tensor_scalar_mul", muY, env.ps_mean[0:1, 0:512], 1.0 / D)
                    bld.sop(V, "tensor_scalar_mul", env.row_tmp[0:1, 0:512],
                            env.ps_sq[0:1, 0:512], 1.0 / D)
                    bld.sop(V, "tensor_mul", env.row_tmp2[0:1, 0:512], muY, muY)
                    bld.sop(V, "tensor_sub", env.row_tmp[0:1, 0:512],
                            env.row_tmp[0:1, 0:512], env.row_tmp2[0:1, 0:512])
                    bld.sop(V, "tensor_scalar_add", env.row_tmp[0:1, 0:512],
                            env.row_tmp[0:1, 0:512], EPS)
                    bld.sop(V, "reciprocal", env.row_tmp2[0:1, 0:512],
                            env.row_tmp[0:1, 0:512], inc=env.ev)
                    bld.wait(Sc, env.ev)
                    bld.sop(Sc, "activation", out=rstdY, in_=env.row_tmp2[0:1, 0:512],
                            func=AF.Sqrt, inc=env.sc)
                    bld.wait(V, env.sc)
                    emit_replicate(bld, env, muY, env.SF32[:, 1024:1536])
                    emit_replicate(bld, env, rstdY, env.SF32[:, 1536:2048])
                for oc in range(NKC):
                    bank = env.ps[oc % 2]
                    bld.wait(T, env.ev)
                    bld.wait(T, env.sc)
                    for kc in range(NKC):
                        bld.op(T, "matmul", out=bank[:, :],
                               lhsT=PJB[:, kc * D + oc * 128: kc * D + (oc + 1) * 128],
                               rhs=env.BIGA[:, kc * LOCT + c0: kc * LOCT + c0 + 512],
                               start=(kc == 0), stop=(kc == NKC - 1),
                               inc=(env.mm if kc == NKC - 1 else ()))
                    bld.wait(Sc, env.mm)
                    bld.op(Sc, "activation", out=yscr[:, :], in_=bank[:, :], func=AF.Identity,
                           bias=t_projb[:, oc:oc + 1], inc=env.sc)
                    xt = env.X[:, oc * LOCT + c0: oc * LOCT + c0 + 512]
                    bld.wait(V, env.sc)
                    bld.wait(V, env.mm)
                    bld.op(V, "tensor_sub", env.qscr[:, :], xt, env.murep[:, 0:512])
                    bld.op(V, "tensor_mul", env.qscr[:, :], env.qscr[:, :], env.rstdrep[:, 0:512])
                    bld.op(V, "tensor_add", env.qscr[:, :], env.qscr[:, :], yscr[:, :], inc=env.ev)
                    if pss == 0:
                        bld.wait(T, env.ev)
                        bld.op(T, "matmul", out=env.ps_mean[0:1, 0:512],
                               lhsT=env.ones_col_f32, rhs=env.qscr[:, :],
                               start=(oc == 0), stop=(oc == NKC - 1),
                               skip_group_check=True,
                               inc=(env.mm if oc == NKC - 1 else ()))
                        bld.wait(Sc, env.ev)
                        bld.op(Sc, "activation", out=env.sq_scr[:, 0:512], in_=env.qscr[:, :],
                               func=AF.Square, inc=env.sc)
                        bld.wait(T, env.sc)
                        bld.op(T, "matmul", out=env.ps_sq[0:1, 0:512],
                               lhsT=env.ones_col_f32, rhs=env.sq_scr[:, 0:512],
                               start=(oc == 0), stop=(oc == NKC - 1),
                               skip_group_check=True, inc=env.mm)
                    else:
                        bld.op(V, "tensor_sub", env.qscr[:, :], env.qscr[:, :],
                               env.SF32[:, 1024:1536])
                        bld.op(V, "tensor_mul", env.qscr[:, :], env.qscr[:, :],
                               env.SF32[:, 1536:2048], inc=env.ev)
                        bld.wait(Sc, env.ev)
                        bld.op(Sc, "activation", out=env.qf[:, :], in_=env.qscr[:, :],
                               func=AF.Identity, bias=t_lnb[:, oc:oc + 1],
                               scale=t_lng[:, oc:oc + 1], inc=env.sc)
                        bld.wait(V, env.sc)
                        bld.op(V, "tensor_add", xt, xt, env.qf[:, :], inc=env.ev)
        bld.barrier()
        bld.op(G, "dma_start", out=xout, in_=env.X[:, :], inc=(env.st, 16))
        bld.wait(G, env.st)
        bld.finalize()
    return nc


def host_shared_attn0(inp):
    d = {}
    ww = np.asarray(inp["L0_attn_W_w"], np.float32)
    wb = np.asarray(inp["L0_attn_W_b"], np.float32)
    d["wqT"] = prep_wT(ww[0:D]); d["wkT"] = prep_wT(ww[D:2 * D]); d["wvT"] = prep_wT(ww[2 * D:3 * D])
    d["qb"] = prep_bias(wb[0:D]); d["kb"] = prep_bias(wb[D:2 * D])
    d["vb_row"] = np.ascontiguousarray(wb[2 * D:3 * D].reshape(1, D))
    d["projT"] = prep_wT(inp["L0_attn_proj_w"])
    d["projb"] = prep_bias(inp["L0_attn_proj_b"])
    d["lng"] = prep_bias(inp["L0_attn_ln_g"])
    d["lnb"] = prep_bias(inp["L0_attn_ln_b"])
    rpb = np.asarray(inp["L0_attn_rpb"], np.float32)
    idx = np.minimum(np.arange(32)[None, :] * 128 + np.arange(128)[:, None], rpb.shape[0] - 1)
    d["rpbm"] = np.ascontiguousarray(rpb[idx, :].transpose(0, 2, 1).reshape(128, 512))
    d["ones_col_f32"] = np.ones((128, 1), np.float32)
    d["ones_row_f32"] = np.ones((1, 128), np.float32)
    d["ones_col_bf"] = np.ones((128, 1), bf)
    h2 = np.zeros((128, 2), bf)
    h2[0:64, 0] = 1
    h2[64:128, 1] = 1
    d["onesh"] = h2
    d["id_bf"] = np.eye(128, dtype=bf)
    d["id_f32"] = np.eye(128, dtype=np.float32)
    return d


def run_phase_attn0(nc, xfull, shared, inp):
    xg = x_global(xfull)
    in_maps = []
    for c in range(NCORE):
        dd = dict(shared)
        dd["xo"] = shard_x(xfull, c)
        dd["xg"] = xg
        dd["skj"] = build_skj(np.asarray(inp["L0_attn_rpb"], np.float32), c)
        in_maps.append(dd)
    res = run_bass_kernel_spmd(nc, in_maps, core_ids=list(range(NCORE)))
    return unshard_x([r["xout"] for r in res.results])


def build_ffn1(debug=False, ext_kv=False):
    nc = bass.Bass("TRN2", target_bir_lowering=False)
    def din(name, shape, dt=F32):
        return nc.dram_tensor(name, shape, dt, kind="ExternalInput").ap()

    xo = din("xo", [128, NKC * LOCT])
    xg = kwT = vwT = kb2 = vb_row = None
    if not ext_kv:
        xg = din("xg", [128, NKC * GT])
        kwT = din("kwT", [128, NKC * D]); vwT = din("vwT", [128, NKC * D])
        kb2 = din("kb2", [128, 8]); vb_row = din("vb_row", [1, D])
    qwT = din("qwT", [128, NKC * D])
    qb2 = din("qb2", [128, 8])
    owT = din("owT", [128, NKC * D]); ob = din("ob", [128, 8])
    nw = din("nw", [128, 8]); nb = din("nb", [128, 8])
    ones_col_f32 = din("ones_col_f32", [128, 1]); ones_row_f32 = din("ones_row_f32", [1, 128])
    ones_col_bf = din("ones_col_bf", [128, 1], BF16); onesh = din("onesh", [128, 2], BF16)
    id_bf = din("id_bf", [128, 128], BF16); id_f32 = din("id_f32", [128, 128])

    dk = dict(kind="ExternalOutput") if debug else {}
    if ext_kv:
        kdram = nc.dram_tensor("kdram", [128, NKC * GT], BF16, kind="ExternalInput").ap()
        v2 = nc.dram_tensor("v2dram", [8, NBLK * B, 128, 128], BF16, kind="ExternalInput").ap()
        kn8 = din("kn8", [1, 8])
    else:
        kdram = nc.dram_tensor("kdram", [128, NKC * GT], BF16, **dk).ap()
        v2 = nc.dram_tensor("v2dram", [8, NBLK * B, 128, 128], BF16, **dk).ap()
    rows_dram = nc.dram_tensor("rowsd", [2, LOCT], F32).ap()
    xout = nc.dram_tensor("xout", [128, NKC * LOCT], F32, kind="ExternalOutput").ap()
    if debug:
        xn_dbg = nc.dram_tensor("xn_dbg", [128, NKC * LOCT], BF16, kind="ExternalOutput").ap()
        at_dbg = nc.dram_tensor("at_dbg", [128, NKC * LOCT], BF16, kind="ExternalOutput").ap()
        x4_dbg = nc.dram_tensor("x4_dbg", [128, NKC * LOCT], F32, kind="ExternalOutput").ap()
        x3_dbg = nc.dram_tensor("x3_dbg", [128, NKC * LOCT], F32, kind="ExternalOutput").ap()

    MQ = [8, 4, 2, 1]

    with ExitStack() as ctx:
        bld = Bld(nc, ctx)
        env = _alloc_common(nc, ctx, bld, sbf_cols=17920)
        E = ctx.enter_context
        t_qb = E(nc.sbuf_tensor("tqb", [128, 8], F32))
        t_kb = E(nc.sbuf_tensor("tkb", [128, 8], F32))
        t_vb = E(nc.sbuf_tensor("tvb", [1, D], F32))
        t_ob = E(nc.sbuf_tensor("tob", [128, 8], F32))
        t_nw = E(nc.sbuf_tensor("tnw", [128, 8], F32))
        t_nb = E(nc.sbuf_tensor("tnb", [128, 8], F32))
        kn2max = E(nc.sbuf_tensor("kn2m", [1, 1], F32))

        _load_x_and_consts(bld, env, xo, [
            (env.c_ones_col_f32[:, :], ones_col_f32), (env.c_ones_row_f32[:, :], ones_row_f32),
            (env.c_ones_col_bf[:, :], ones_col_bf), (env.c_onesh[:, :], onesh),
            (env.c_id_bf[:, :], id_bf), (env.c_id_f32[:, :], id_f32),
            (t_qb[:, :], qb2),
            (t_ob[:, :], ob), (t_nw[:, :], nw), (t_nb[:, :], nb)] +
            ([] if ext_kv else [(t_kb[:, :], kb2), (t_vb[:, :], vb_row)]))
        bld.op(V, "tensor_scalar_mul", t_qb[:, :], t_qb[:, :], 1.0 / 32.0)
        bld.op(V, "tensor_scalar_mul", t_ob[:, :], t_ob[:, :], 0.5)
        bld.op(V, "memset", kn2max[:, :], 0.0, inc=env.ev)
        WKB = WVB = None
        if ext_kv:
            t_kn8 = E(nc.sbuf_tensor("tkn8", [1, 8], F32))
            bld.op(Y, "dma_start", out=t_kn8[:, :], in_=kn8, inc=(env.ld, 16))
            bld.wait(V, env.ld)
            bld.sop(V, "tensor_reduce", kn2max[:, :], t_kn8[:, :], AX.X, ALU.max,
                    inc=env.ev)
        else:
            WKB = env.SBF[:, 0:8192]
            WVB = env.SBF[:, 8192:16384]
            emit_load_cast(bld, env, lambda c0, c1: kwT[:, c0:c1], env.stage2, WKB, NKC * D, 1024)
            emit_load_cast(bld, env, lambda c0, c1: vwT[:, c0:c1], env.stage2, WVB, NKC * D, 1024)
        bld.barrier()

        XEF = env.BIGA[:, 0:8192].bitcast(F32)
        for g in range(4):
            c0 = g * 512
            emit_ln_group(
                bld, env,
                xsrc=lambda kc, c0=c0: env.X[:, kc * LOCT + c0: kc * LOCT + c0 + 512],
                nkc=NKC, ncols=512, mu_row=env.mu_g, rstd_row=env.rstd_g,
                xn_dst=lambda kc: XEF[:, kc * 512:(kc + 1) * 512])
            bld.wait(G, env.ev)
            bld.wait(G, env.sc)
            bld.op(G, "dma_start", out=rows_dram[0:1, c0:c0 + 512], in_=env.mu_g,
                   inc=(env.gc, 16))
            bld.op(G, "dma_start", out=rows_dram[1:2, c0:c0 + 512], in_=env.rstd_g,
                   inc=(env.gc, 16))
            bld.wait(V, env.gc)
            emit_ln_stats(bld, env, lambda kc: XEF[:, kc * 512:(kc + 1) * 512],
                          NKC, 512, env.mu_p, env.rstd_p)
            bld.wait(V, env.sc)
            emit_replicate(bld, env, env.mu_p, env.murep[:, 0:512])
            emit_replicate(bld, env, env.rstd_p, env.rstdrep[:, 0:512])
            for j in range(4):
                t0 = c0 + j * 128
                for kc in range(NKC):
                    dstt = env.XN[:, kc * LOCT + t0: kc * LOCT + t0 + 128]
                    if kc >= MQ[j]:
                        bld.op(V, "memset", dstt, 0.0, inc=env.ev)
                        continue
                    bld.wait(V, env.sc)
                    bld.op(V, "tensor_sub", env.vec_scr[:, 0:128],
                           XEF[:, kc * 512 + j * 128: kc * 512 + (j + 1) * 128],
                           env.murep[:, j * 128:(j + 1) * 128])
                    bld.op(V, "tensor_mul", env.vec_scr[:, 0:128], env.vec_scr[:, 0:128],
                           env.rstdrep[:, j * 128:(j + 1) * 128], inc=env.ev)
                    bld.wait(Sc, env.ev)
                    bld.op(Sc, "activation", out=dstt, in_=env.vec_scr[:, 0:128],
                           func=AF.Identity, bias=t_nb[:, kc:kc + 1],
                           scale=t_nw[:, kc:kc + 1], inc=env.sc)
        bld.barrier()

        if debug:
            bld.op(G, "dma_start", out=xn_dbg, in_=env.XN[:, :], inc=(env.gc, 16))
        env.xng = env.BIGA[:, 8192:12288]
        cfg = Env()
        cfg.dh = 128
        cfg.n_tg = GT // 512
        cfg.xsrc_dram = xg
        cfg.rope = False
        cfg.WKB, cfg.WVB = WKB, WVB
        cfg.t_kb, cfg.t_vb = t_kb, t_vb
        cfg.kdram, cfg.v2 = kdram, v2
        cfg.kd_t = GT
        cfg.kn2max = kn2max
        cfg.onesh = env.c_onesh
        cfg.ones_col_bf = env.c_ones_col_bf[:, 0:1]
        cfg.kbf = env.SBF[:, 16384:16896]
        cfg.vbf = env.SBF[:, 16896:17408]
        env.sqb = env.SBF[:, 17408:17920]
        cfg.kstore, cfg.vstore = {}, {}

        def xn_maker(bld_, env_, tg):
            qtr = (tg % 8) // 2
            mq = MQ[qtr]
            emit_ln_group(bld_, env_,
                          xsrc=lambda kc: env_.XG[:, kc * 512:(kc + 1) * 512],
                          nkc=NKC, ncols=512, mu_row=env_.mu_p, rstd_row=env_.rstd_p,
                          xn_dst=lambda kc: XEF[:, kc * 512:(kc + 1) * 512])
            emit_ln_stats(bld_, env_, lambda kc: XEF[:, kc * 512:(kc + 1) * 512],
                          NKC, 512, env_.mu_g, env_.rstd_g)
            bld_.wait(V, env_.sc)
            emit_replicate(bld_, env_, env_.mu_g, env_.murep[:, 0:512])
            emit_replicate(bld_, env_, env_.rstd_g, env_.rstdrep[:, 0:512])
            for kc in range(NKC):
                dstt = env_.xng[:, kc * 512:(kc + 1) * 512]
                if kc >= mq:
                    bld_.op(V, "memset", dstt, 0.0, inc=env_.ev)
                    continue
                bld_.wait(V, env_.sc)
                bld_.op(V, "tensor_sub", env_.vec_scr[:, 0:512],
                        XEF[:, kc * 512:(kc + 1) * 512], env_.murep[:, 0:512])
                bld_.op(V, "tensor_mul", env_.vec_scr[:, 0:512], env_.vec_scr[:, 0:512],
                        env_.rstdrep[:, 0:512], inc=env_.ev)
                bld_.wait(Sc, env_.ev)
                bld_.op(Sc, "activation", out=dstt, in_=env_.vec_scr[:, 0:512],
                        func=AF.Identity, bias=t_nb[:, kc:kc + 1],
                        scale=t_nw[:, kc:kc + 1], inc=env_.sc)
        cfg.xn_maker = xn_maker
        if not ext_kv:
            emit_kv_prologue(bld, env, cfg)

        acfg = Env()
        acfg.heads, acfg.dh = 8, 128
        acfg.scale = 1.0 / 32.0
        acfg.rope = False
        acfg.skj = None
        acfg.bmax_row = None
        acfg.l_ones = False
        acfg.vp_bufs = 1
        acfg.kdram, acfg.v2 = kdram, v2
        acfg.kd_t = GT
        acfg.wqT = qwT
        acfg.t_qb = t_qb
        acfg.kn2max = kn2max
        acfg.ones_col_bf = env.c_ones_col_bf
        acfg.id_bf = env.c_id_bf[:, :]
        env.VP = env.SBF[:, 0:4096]
        env.KBuf = env.SBF[:, 4096:8192]
        env.PT = [env.SBF[:, 8192 + i * 512:8192 + (i + 1) * 512] for i in range(2)]
        env.WQ = [env.SBF[:, 9216 + i * 1024:9216 + (i + 1) * 1024] for i in range(2)]
        env.qbh = env.SBF[:, 11264:11776]
        env.sqq = env.SBF[:, 11776:12288]
        bld.barrier()
        emit_attention(bld, env, acfg)
        if debug:
            bld.op(G, "dma_start", out=at_dbg, in_=env.BIGA[:, :], inc=(env.gc, 16))
            bld.op(G, "dma_start", out=x3_dbg, in_=env.X[:, :], inc=(env.gc, 16))
            bld.wait(G, env.gc)
            bld.barrier()

        for g in range(4):
            c0 = g * 512
            bld.wait(Y, env.ev)
            bld.op(Y, "dma_start", out=env.mu_g, in_=rows_dram[0:1, c0:c0 + 512], inc=(env.ld, 16))
            bld.op(Y, "dma_start", out=env.rstd_g, in_=rows_dram[1:2, c0:c0 + 512], inc=(env.ld, 16))
            bld.wait(V, env.ld)
            bld.sop(V, "tensor_scalar_mul", env.rstd_g, env.rstd_g, 0.5, inc=env.ev)
            emit_replicate(bld, env, env.mu_g, env.murep[:, 0:512])
            emit_replicate(bld, env, env.rstd_g, env.rstdrep[:, 0:512])
            for oc in range(NKC):
                xt = env.X[:, oc * LOCT + c0: oc * LOCT + c0 + 512]
                bld.op(V, "tensor_sub", env.qscr[:, :], xt, env.murep[:, 0:512])
                bld.op(V, "tensor_mul", env.qf[:, :], env.qscr[:, :], env.rstdrep[:, 0:512])
                bld.op(V, "tensor_add", env.qscr[:, :], xt, env.qf[:, :])
                bld.op(V, "tensor_copy", xt, env.qscr[:, :], inc=env.ev)
            bld.barrier()
        if debug:
            bld.op(G, "dma_start", out=x4_dbg, in_=env.X[:, :], inc=(env.gc, 16))
            bld.wait(G, env.gc)
            bld.barrier()

        OWB = env.SBF[:, 0:8192]
        emit_load_cast(bld, env, lambda c0, c1: owT[:, c0:c1], env.stage2, OWB, NKC * D, 1024)
        bld.barrier()
        for g in range(4):
            for j in range(4):
                t0 = g * 512 + j * 128
                mq = MQ[j]
                for oc in range(mq):
                    bank = env.ps[oc % 2]
                    bld.wait(T, env.ev)
                    bld.wait(T, env.sc)
                    for kc in range(mq):
                        bld.op(T, "matmul", out=bank[:, 0:128],
                               lhsT=OWB[:, kc * D + oc * 128: kc * D + (oc + 1) * 128],
                               rhs=env.BIGA[:, kc * LOCT + t0: kc * LOCT + t0 + 128],
                               start=(kc == 0), stop=(kc == mq - 1),
                               inc=(env.mm if kc == mq - 1 else ()))
                    bld.wait(Sc, env.mm)
                    bld.op(Sc, "activation", out=env.qf[:, 0:128], in_=bank[:, 0:128],
                           func=AF.Identity, bias=t_ob[:, oc:oc + 1], scale=0.5, inc=env.sc)
                    xt = env.X[:, oc * LOCT + t0: oc * LOCT + t0 + 128]
                    bld.wait(V, env.sc)
                    bld.op(V, "tensor_add", xt, xt, env.qf[:, 0:128], inc=env.ev)
        bld.barrier()
        bld.op(G, "dma_start", out=xout, in_=env.X[:, :], inc=(env.st, 16))
        bld.wait(G, env.st)
        bld.finalize()
    return nc


def host_shared_ffn1(inp):
    d = {}
    d["qwT"] = prep_wT(inp["L1_ffn_q_w"]); d["kwT"] = prep_wT(inp["L1_ffn_k_w"])
    d["vwT"] = prep_wT(inp["L1_ffn_v_w"]); d["owT"] = prep_wT(inp["L1_ffn_o_w"])
    d["qb2"] = prep_bias(inp["L1_ffn_q_b"]); d["kb2"] = prep_bias(inp["L1_ffn_k_b"])
    d["vb_row"] = np.ascontiguousarray(np.asarray(inp["L1_ffn_v_b"], np.float32).reshape(1, D))
    d["ob"] = prep_bias(inp["L1_ffn_o_b"])
    d["nw"] = prep_bias(inp["L1_ffn_norm_w"]); d["nb"] = prep_bias(inp["L1_ffn_norm_b"])
    d["ones_col_f32"] = np.ones((128, 1), np.float32)
    d["ones_row_f32"] = np.ones((1, 128), np.float32)
    d["ones_col_bf"] = np.ones((128, 1), bf)
    h2 = np.zeros((128, 2), bf)
    h2[0:64, 0] = 1; h2[64:128, 1] = 1
    d["onesh"] = h2
    d["id_bf"] = np.eye(128, dtype=bf)
    d["id_f32"] = np.eye(128, dtype=np.float32)
    return d


def run_phase_ffn1(nc, xfull, shared):
    xg = x_global(xfull)
    in_maps = []
    for c in range(NCORE):
        dd = dict(shared)
        dd["xo"] = shard_x(xfull, c)
        dd["xg"] = xg
        in_maps.append(dd)
    res = run_bass_kernel_spmd(nc, in_maps, core_ids=list(range(NCORE)))
    return unshard_x([r["xout"] for r in res.results])


def build_attn1(debug=False):
    nc = bass.Bass("TRN2", target_bir_lowering=False)
    def din(name, shape, dt=F32):
        return nc.dram_tensor(name, shape, dt, kind="ExternalInput").ap()

    xo = din("xo", [128, NKC * LOCT])
    wuT = din("wuT", [128, NKC * 2048]); wvT = din("wvT", [128, NKC * 2048])
    wbT = din("wbT", [128, NKC * 128]); woT = din("woT", [128, 16 * D])
    bu = din("bu", [128, 16]); bv = din("bv", [128, 16]); bb2 = din("bb2", [128, 1])
    bo = din("bo", [128, 8])
    lng = din("lng", [128, 8]); lnb = din("lnb", [128, 8])
    qqwv = din("qqwv", [128, 4])
    ropei = din("ropei", [128, 8])
    bias16 = din("bias16", [1, 16])
    ones_col_f32 = din("ones_col_f32", [128, 1]); ones_row_f32 = din("ones_row_f32", [1, 128])
    id_f32 = din("id_f32", [128, 128])

    rows_dram = nc.dram_tensor("rowsd", [2, LOCT], F32).ap()
    dk = dict(kind="ExternalOutput") if debug else {}
    kern_dram = nc.dram_tensor("kernd", [10, 512], F32, **dk).ap()
    xout = nc.dram_tensor("xout", [128, NKC * LOCT], F32, kind="ExternalOutput").ap()
    if debug:
        xn_dbg = nc.dram_tensor("xn_dbg", [128, NKC * LOCT], BF16, kind="ExternalOutput").ap()
        xa_dbg = nc.dram_tensor("xa_dbg", [128, 2 * NKC * LOCT], BF16, kind="ExternalOutput").ap()
        qq_dbg = nc.dram_tensor("qq_dbg", [128, 4096], F32, kind="ExternalOutput").ap()
        kr_dbg = nc.dram_tensor("kr_dbg", [128, 1024], F32, kind="ExternalOutput").ap()

    PAIRS = [(i, j) for i in range(4) for j in range(i + 1)]

    with ExitStack() as ctx:
        bld = Bld(nc, ctx)
        env = _alloc_common(nc, ctx, bld, sbf_cols=16384)
        E = ctx.enter_context
        t_bu = E(nc.sbuf_tensor("tbu", [128, 16], F32))
        t_bv = E(nc.sbuf_tensor("tbv", [128, 16], F32))
        t_bb = E(nc.sbuf_tensor("tbb", [128, 1], F32))
        t_bo = E(nc.sbuf_tensor("tbo", [128, 8], F32))
        t_lng = E(nc.sbuf_tensor("tlng", [128, 8], F32))
        t_lnb = E(nc.sbuf_tensor("tlnb", [128, 8], F32))
        t_qq = E(nc.sbuf_tensor("tqq", [128, 4], F32))
        t_ri = E(nc.sbuf_tensor("tri", [128, 8], F32))
        t_b16 = E(nc.sbuf_tensor("tb16", [1, 16], F32))
        WUB = E(nc.sbuf_tensor("wub", [128, 1024], BF16))
        WVB2 = E(nc.sbuf_tensor("wvb2", [128, 1024], BF16))
        WBB = WUB
        WOB = E(nc.sbuf_tensor("wob", [128, 1024], BF16))
        kern_row = env.rows[0:1, 2048:2560]
        kern_row2 = env.rows[0:1, 2560:3072]

        _load_x_and_consts(bld, env, xo, [
            (env.c_ones_col_f32[:, :], ones_col_f32), (env.c_ones_row_f32[:, :], ones_row_f32),
            (env.c_id_f32[:, :], id_f32),
            (t_bu[:, :], bu), (t_bv[:, :], bv), (t_bb[:, :], bb2), (t_bo[:, :], bo),
            (t_lng[:, :], lng), (t_lnb[:, :], lnb), (t_qq[:, :], qqwv),
            (t_ri[:, :], ropei), (t_b16[:, :], bias16)])
        emit_load_cast(bld, env, lambda c0, c1: wbT[:, c0:c1], env.stage2, WBB, NKC * 128, 1024)
        bld.barrier()

        XQF = env.SF32[:, 0:4096]
        for g in range(4):
            c0 = g * 512
            emit_ln_group(
                bld, env,
                xsrc=lambda kc, c0=c0: env.X[:, kc * LOCT + c0: kc * LOCT + c0 + 512],
                nkc=NKC, ncols=512, mu_row=env.mu_g, rstd_row=env.rstd_g,
                xn_dst=lambda kc: XQF[:, kc * 512:(kc + 1) * 512])
            bld.wait(G, env.ev)
            bld.wait(G, env.sc)
            bld.op(G, "dma_start", out=rows_dram[0:1, c0:c0 + 512], in_=env.mu_g,
                   inc=(env.gc, 16))
            bld.op(G, "dma_start", out=rows_dram[1:2, c0:c0 + 512], in_=env.rstd_g,
                   inc=(env.gc, 16))
            bld.wait(V, env.gc)
            emit_ln_group(
                bld, env,
                xsrc=lambda kc: XQF[:, kc * 512:(kc + 1) * 512],
                nkc=NKC, ncols=512, mu_row=env.mu_p, rstd_row=env.rstd_p,
                xn_dst=lambda kc, c0=c0: env.XN[:, kc * LOCT + c0: kc * LOCT + c0 + 512],
                gain_bias=lambda kc: (t_lng[:, kc:kc + 1], t_lnb[:, kc:kc + 1]))
        bld.barrier()

        quadq = env.SF32[:, 0:2048]
        quadk = env.SF32[:, 2048:4096]
        base_g = env.SF32[:, 4096:4608]
        tq = env.SF32[:, 4608:5120]
        rot = env.SF32[:, 5120:5632]
        for g in range(4):
            c0 = g * 512
            bank = env.ps[g % 2]
            bld.wait(T, env.ev)
            bld.wait(T, env.sc)
            for kc in range(NKC):
                bld.op(T, "matmul", out=bank[:, :],
                       lhsT=WBB[:, kc * 128:(kc + 1) * 128],
                       rhs=env.XN[:, kc * LOCT + c0: kc * LOCT + c0 + 512],
                       start=(kc == 0), stop=(kc == NKC - 1),
                       inc=(env.mm if kc == NKC - 1 else ()))
            bld.wait(Sc, env.mm)
            bld.op(Sc, "activation", out=base_g[:, :], in_=bank[:, :], func=AF.Silu,
                   bias=t_bb[:, 0:1], inc=env.sc)
            for which, (wi, bi2) in enumerate([(0, 1), (2, 3)]):
                dst = (quadq if which == 0 else quadk)[:, c0:c0 + 512]
                bld.wait(V, env.sc)
                bld.wait(V, env.gcast)
                bld.op(V, "tensor_scalar", tq[:, :], base_g[:, :],
                       t_qq[:, wi:wi + 1], t_qq[:, bi2:bi2 + 1],
                       ALU.mult, ALU.add, inc=env.ev)
                bld.wait(G, env.ev)
                bld.op(G, "tensor_copy", out=rot[0:64, :], in_=tq[64:128, :])
                bld.op(G, "tensor_copy", out=rot[64:128, :], in_=tq[0:64, :], inc=env.gcast)
                bld.wait(V, env.gcast)
                bld.op(V, "tensor_scalar", dst, tq[:, :], t_ri[:, 2 * g:2 * g + 1], None,
                       ALU.mult)
                bld.op(V, "tensor_scalar", rot[:, :], rot[:, :], t_ri[:, 2 * g + 1:2 * g + 2],
                       None, ALU.mult)
                bld.op(V, "tensor_add", dst, dst, rot[:, :], inc=env.ev)
        bld.barrier()
        if debug:
            bld.op(G, "dma_start", out=xn_dbg, in_=env.XN[:, :], inc=(env.gc, 16))
            bld.op(G, "dma_start", out=qq_dbg, in_=env.SF32[:, 0:4096], inc=(env.gc, 16))
            bld.wait(G, env.gc)
            bld.barrier()
        for pidx, (i, j) in enumerate(PAIRS):
            bld.wait(V, env.mm)
            bld.op(V, "tensor_mul", env.vec_scr[:, 0:512],
                   quadq[:, i * 512:(i + 1) * 512], quadk[:, j * 512:(j + 1) * 512],
                   inc=env.ev)
            bld.wait(T, env.ev)
            bld.op(T, "matmul", out=env.ps[4][0:1, :], lhsT=env.ones_col_f32,
                   rhs=env.vec_scr[:, 0:512], start=True, stop=True, inc=env.mm)
            bld.wait(V, env.mm)
            bld.sop(V, "tensor_scalar", kern_row, env.ps[4][0:1, :],
                    1.0 / 64.0, t_b16[0:1, pidx:pidx + 1], ALU.mult, ALU.add)
            bld.sop(V, "tensor_scalar_max", kern_row2, kern_row, 0.0)
            bld.sop(V, "tensor_mul", kern_row, kern_row2, kern_row2, inc=env.ev)
            bld.wait(G, env.ev)
            bld.op(G, "dma_start", out=kern_dram[pidx:pidx + 1, :], in_=kern_row,
                   inc=(env.gc, 16))
        bld.barrier()

        uch = env.SF32[:, 0:2048]
        vch = env.SF32[:, 2048:4096]
        t_quad = E(nc.sbuf_tensor("tquad", [128, 512], BF16))
        t_krep = E(nc.sbuf_tensor("tkrep", [128, 512], BF16))
        quad = t_quad[:, :]
        kernrep = t_krep[:, :]

        def xat(ch):
            return env.BIGA[:, ch * 2048:(ch + 1) * 2048] if ch < 8 \
                else env.SBF[:, (ch - 8) * 2048:(ch - 7) * 2048]

        wstage_done = []
        for ch in range(16):
            bld.wait(Y, env.gcast)
            bld.op(Y, "dma_start",
                   out=env.stage2[0][:, :].rearrange("p (k c) -> p k c", c=128),
                   in_=wuT.rearrange("p (k c) -> p k c", c=2048)[:, :, ch * 128:(ch + 1) * 128],
                   inc=(env.ld, 16))
            bld.op(Y, "dma_start",
                   out=env.stage2[1][:, :].rearrange("p (k c) -> p k c", c=128),
                   in_=wvT.rearrange("p (k c) -> p k c", c=2048)[:, :, ch * 128:(ch + 1) * 128],
                   inc=(env.ld, 16))
            bld.wait(G, env.ld)
            bld.wait(G, env.mm)
            bld.op(G, "tensor_copy", out=WUB[:, :], in_=env.stage2[0][:, :])
            bld.op(G, "tensor_copy", out=WVB2[:, :], in_=env.stage2[1][:, :], inc=env.gcast)
            for g in range(4):
                c0 = g * 512
                for which in range(2):
                    bank = env.ps[4 + (g * 2 + which) % 4]
                    wbuf = WUB if which == 0 else WVB2
                    tb = t_bu if which == 0 else t_bv
                    dst = (uch if which == 0 else vch)[:, c0:c0 + 512]
                    bld.wait(T, env.sc)
                    bld.wait(T, env.gcast)
                    for kc in range(NKC):
                        bld.op(T, "matmul", out=bank[:, :],
                               lhsT=wbuf[:, kc * 128:(kc + 1) * 128],
                               rhs=env.XN[:, kc * LOCT + c0: kc * LOCT + c0 + 512],
                               start=(kc == 0), stop=(kc == NKC - 1),
                               inc=(env.mm if kc == NKC - 1 else ()))
                    bld.wait(Sc, env.mm)
                    bld.wait(Sc, env.ev)
                    bld.op(Sc, "activation", out=dst, in_=bank[:, :], func=AF.Silu,
                           bias=tb[:, ch:ch + 1], inc=env.sc)
            bld.wait(Y, env.gc)
            for i in range(4):
                first = True
                for j in range(i + 1):
                    pidx = PAIRS.index((i, j))
                    bld.wait(Y, env.ev)
                    bld.op(Y, "dma_start", out=kern_row,
                           in_=kern_dram[pidx:pidx + 1, :], inc=(env.ld, 16))
                    bld.wait(T, env.ld)
                    bld.wait(T, env.ev)
                    bld.op(T, "matmul", out=env.ps_rep[:, :], lhsT=env.ones_row_f32,
                           rhs=kern_row, start=True, stop=True, inc=env.mm)
                    bld.wait(V, env.mm)
                    bld.wait(V, env.sc)
                    bld.op(V, "tensor_copy", out=kernrep[:, :], in_=env.ps_rep[:, :],
                           inc=env.ev)
                    if first:
                        bld.op(V, "tensor_mul", quad[:, :], kernrep[:, :],
                               vch[:, j * 512:(j + 1) * 512], inc=env.ev)
                        first = False
                    else:
                        bld.op(V, "tensor_mul", env.vec_scr[:, 0:512], kernrep[:, :],
                               vch[:, j * 512:(j + 1) * 512])
                        bld.op(V, "tensor_add", quad[:, :], quad[:, :],
                               env.vec_scr[:, 0:512], inc=env.ev)
                bld.op(V, "tensor_mul", xat(ch)[:, i * 512:(i + 1) * 512],
                       uch[:, i * 512:(i + 1) * 512], quad[:, :], inc=env.ev)
        bld.barrier()
        if debug:
            bld.op(G, "dma_start", out=kr_dbg[:, 0:512], in_=kernrep[:, :], inc=(env.gc, 16))
            bld.op(G, "dma_start", out=kr_dbg[:, 512:1024], in_=quad[:, :], inc=(env.gc, 16))
            bld.op(G, "dma_start", out=xa_dbg[:, 0:NKC * LOCT], in_=env.BIGA[:, :], inc=(env.gc, 16))
            bld.op(G, "dma_start", out=xa_dbg[:, NKC * LOCT:], in_=env.SBF[:, 0:16384], inc=(env.gc, 16))
            bld.wait(G, env.gc)
            bld.barrier()

        wo_done = []
        def proj_emit(bld_, env_, oc, g, bank):
            c0 = g * 512
            for hh in range(2):
                bld_.wait(Y, env_.gcast)
                bld_.op(Y, "dma_start",
                        out=env_.stage2[0][:, :].rearrange("p (k c) -> p k c", c=128),
                        in_=woT.rearrange("p (k c) -> p k c", c=D)
                            [:, hh * 8:(hh + 1) * 8, oc * 128:(oc + 1) * 128],
                        inc=(env_.ld, 16))
                bld_.wait(G, env_.ld)
                bld_.wait(G, env_.mm, wo_done[-1] if wo_done else 0)
                bld_.op(G, "tensor_copy", out=WOB[:, :], in_=env_.stage2[0][:, :],
                        inc=env_.gcast)
                bld_.wait(T, env_.gcast)
                for kc8 in range(8):
                    kc = hh * 8 + kc8
                    bld_.op(T, "matmul", out=bank[:, :],
                            lhsT=WOB[:, kc8 * 128:(kc8 + 1) * 128],
                            rhs=xat(kc)[:, c0:c0 + 512],
                            start=(kc == 0), stop=(kc == 15),
                            inc=(env_.mm if kc8 == 7 else ()))
                wo_done.append(env_.mm.n)
        _residual_epilogue(bld, env, rows_dram, proj_emit, t_bo, halfed=False)

        bld.op(G, "dma_start", out=xout, in_=env.X[:, :], inc=(env.st, 16))
        bld.wait(G, env.st)
        bld.finalize()
    return nc


def host_shared_attn1(inp):
    d = {}
    d["wuT"] = prep_wT(inp["L1_attn_Wu"]); d["wvT"] = prep_wT(inp["L1_attn_Wv"])
    d["wbT"] = prep_wT(inp["L1_attn_Wb"]); d["woT"] = prep_wT(inp["L1_attn_Wo"])
    d["bu"] = prep_bias(inp["L1_attn_bu"]); d["bv"] = prep_bias(inp["L1_attn_bv"])
    d["bb2"] = prep_bias(inp["L1_attn_bb"]); d["bo"] = prep_bias(inp["L1_attn_bo"])
    d["lng"] = prep_bias(inp["L1_attn_ln_g"]); d["lnb"] = prep_bias(inp["L1_attn_ln_b"])
    qq = np.stack([np.asarray(inp["L1_attn_qqw"], np.float32),
                   np.asarray(inp["L1_attn_qqb"], np.float32),
                   np.asarray(inp["L1_attn_qkw"], np.float32),
                   np.asarray(inp["L1_attn_qkb"], np.float32)], axis=1)
    d["qqwv"] = np.ascontiguousarray(qq)
    half = 64
    invf = 10000.0 ** (-np.arange(half) / half)
    ri = np.zeros((128, 8), np.float32)
    for i in range(4):
        c = np.cos(i * invf); s_ = np.sin(i * invf)
        ri[0:64, 2 * i] = c; ri[64:128, 2 * i] = c
        ri[0:64, 2 * i + 1] = -s_; ri[64:128, 2 * i + 1] = s_
    d["ropei"] = ri
    wr = np.asarray(inp["L1_attn_w_rel"], np.float32)
    b16 = np.zeros((1, 16), np.float32)
    for pidx, (i, j) in enumerate([(i, j) for i in range(4) for j in range(i + 1)]):
        b16[0, pidx] = wr[511 + j - i]
    d["bias16"] = b16
    d["ones_col_f32"] = np.ones((128, 1), np.float32)
    d["ones_row_f32"] = np.ones((1, 128), np.float32)
    d["id_f32"] = np.eye(128, dtype=np.float32)
    return d


def run_phase_attn1(nc, xfull, shared):
    in_maps = []
    for c in range(NCORE):
        dd = dict(shared)
        dd["xo"] = shard_x(xfull, c)
        in_maps.append(dd)
    res = run_bass_kernel_spmd(nc, in_maps, core_ids=list(range(NCORE)))
    return unshard_x([r["xout"] for r in res.results])




def build_kv(kind):
    dh = 128 if kind == "ffn1" else 64
    nh = 8 if kind == "ffn1" else 16
    rope = (kind == "ffn0")
    nc = bass.Bass("TRN2", target_bir_lowering=False)
    def din(name, shape, dt=F32):
        return nc.dram_tensor(name, shape, dt, kind="ExternalInput").ap()
    xo = din("xo", [128, NKC * LOCT])
    wkT = din("wkT", [128, NKC * D]); wvT = din("wvT", [128, NKC * D])
    kb = din("kb", [128, 8]); vb_row = din("vb_row", [1, D])
    ones_col_f32 = din("ones_col_f32", [128, 1]); ones_row_f32 = din("ones_row_f32", [1, 128])
    ones_col_bf = din("ones_col_bf", [128, 1], BF16); onesh = din("onesh", [128, 2], BF16)
    if rope:
        rc_glob = din("rc_glob", [128, LOCT]); rs_glob = din("rs_glob", [128, LOCT])
    if kind == "ffn1":
        nw = din("nw", [128, 8]); nb = din("nb", [128, 8])
    kloc = nc.dram_tensor("kloc", [128, NKC * LOCT], BF16, kind="ExternalOutput").ap()
    v2loc = nc.dram_tensor("v2loc", [nh, 16, 128, dh], BF16, kind="ExternalOutput").ap()
    kn2out = nc.dram_tensor("kn2out", [1, 1], F32, kind="ExternalOutput").ap()
    MQ = [8, 4, 2, 1]
    with ExitStack() as ctx:
        bld = Bld(nc, ctx)
        env = _alloc_common(nc, ctx, bld, sbf_cols=17920)
        E = ctx.enter_context
        t_kb = E(nc.sbuf_tensor("tkb", [128, 8], F32))
        t_vb = E(nc.sbuf_tensor("tvb", [1, D], F32))
        kn2max = E(nc.sbuf_tensor("kn2m", [1, 1], F32))
        pairs = [(env.c_ones_col_f32[:, :], ones_col_f32),
                 (env.c_ones_row_f32[:, :], ones_row_f32),
                 (env.c_ones_col_bf[:, :], ones_col_bf), (env.c_onesh[:, :], onesh),
                 (t_kb[:, :], kb), (t_vb[:, :], vb_row)]
        if kind == "ffn1":
            t_nw = E(nc.sbuf_tensor("tnw", [128, 8], F32))
            t_nb = E(nc.sbuf_tensor("tnb", [128, 8], F32))
            pairs += [(t_nw[:, :], nw), (t_nb[:, :], nb)]
        _load_x_and_consts(bld, env, xo, pairs)
        bld.op(V, "memset", kn2max[:, :], 0.0, inc=env.ev)
        WKB = env.SBF[:, 0:8192]
        WVB = env.SBF[:, 8192:16384]
        emit_load_cast(bld, env, lambda c0, c1: wkT[:, c0:c1], env.stage2, WKB, NKC * D, 1024)
        emit_load_cast(bld, env, lambda c0, c1: wvT[:, c0:c1], env.stage2, WVB, NKC * D, 1024)
        bld.barrier()
        env.xng = env.BIGA[:, 0:4096]
        XEF = env.BIGA[:, 8192:16384].bitcast(F32)
        cfg = Env()
        cfg.local = True
        cfg.dh = dh
        cfg.n_tg = 4
        cfg.rope = rope
        if rope:
            cfg.rc_glob, cfg.rs_glob = rc_glob, rs_glob
        cfg.WKB, cfg.WVB = WKB, WVB
        cfg.t_kb, cfg.t_vb = t_kb, t_vb
        cfg.kdram, cfg.v2 = kloc, v2loc
        cfg.kd_t = LOCT
        cfg.kn2max = kn2max
        cfg.onesh = env.c_onesh
        cfg.ones_col_bf = env.c_ones_col_bf[:, 0:1]
        cfg.kbf = env.SBF[:, 16384:16896]
        cfg.vbf = env.SBF[:, 16896:17408]
        env.sqb = env.SBF[:, 17408:17920]
        cfg.kstore, cfg.vstore = {}, {}

        def xn_maker(bld_, env_, tg):
            c0 = tg * 512
            emit_ln_group(bld_, env_,
                          xsrc=lambda kc, c0=c0: env_.X[:, kc * LOCT + c0: kc * LOCT + c0 + 512],
                          nkc=NKC, ncols=512, mu_row=env_.mu_p, rstd_row=env_.rstd_p,
                          xn_dst=(lambda kc: env_.xng[:, kc * 512:(kc + 1) * 512])
                          if kind != "ffn1" else (lambda kc: XEF[:, kc * 512:(kc + 1) * 512]))
            if kind == "ffn1":
                emit_ln_stats(bld_, env_, lambda kc: XEF[:, kc * 512:(kc + 1) * 512],
                              NKC, 512, env_.mu_g, env_.rstd_g)
                bld_.wait(V, env_.sc)
                emit_replicate(bld_, env_, env_.mu_g, env_.murep[:, 0:512])
                emit_replicate(bld_, env_, env_.rstd_g, env_.rstdrep[:, 0:512])
                for j in range(4):
                    mq = MQ[j]
                    for kc in range(NKC):
                        dstt = env_.xng[:, kc * 512 + j * 128: kc * 512 + (j + 1) * 128]
                        if kc >= mq:
                            bld_.op(V, "memset", dstt, 0.0, inc=env_.ev)
                            continue
                        bld_.wait(V, env_.sc)
                        bld_.op(V, "tensor_sub", env_.vec_scr[:, 0:128],
                                XEF[:, kc * 512 + j * 128: kc * 512 + (j + 1) * 128],
                                env_.murep[:, j * 128:(j + 1) * 128])
                        bld_.op(V, "tensor_mul", env_.vec_scr[:, 0:128],
                                env_.vec_scr[:, 0:128],
                                env_.rstdrep[:, j * 128:(j + 1) * 128], inc=env_.ev)
                        bld_.wait(Sc, env_.ev)
                        bld_.op(Sc, "activation", out=dstt, in_=env_.vec_scr[:, 0:128],
                                func=AF.Identity, bias=t_nb[:, kc:kc + 1],
                                scale=t_nw[:, kc:kc + 1], inc=env_.sc)
        cfg.xn_maker = xn_maker
        emit_kv_prologue(bld, env, cfg)
        bld.wait(G, env.ev)
        bld.op(G, "dma_start", out=kn2out, in_=kn2max[:, :], inc=(env.st, 16))
        bld.wait(G, env.st)
        bld.wait(G, env.gc)
        bld.finalize()
    return nc


def gather_kv(kparts, vparts, dh, nh):
    kfull = np.empty((128, NKC, GT), dtype=bf)
    v2full = np.empty((nh, NBLK * B, 128, dh), dtype=bf)
    for c in range(NCORE):
        ob = owned_blocks(c)
        kl = np.asarray(kparts[c]).reshape(128, NKC, 16, 128)
        vl = np.asarray(vparts[c])
        for t in range(16):
            b_, j = divmod(t, 4)
            gblk = b_ * NBLK + ob[j]
            kfull[:, :, gblk * 128:(gblk + 1) * 128] = kl[:, :, t]
            v2full[:, gblk] = vl[:, t]
    return np.ascontiguousarray(kfull.reshape(128, NKC * GT)), np.ascontiguousarray(v2full)



_PROGS = {}


def _prog(name, builder, **kw):
    if name not in _PROGS:
        _PROGS[name] = builder(**kw)
    return _PROGS[name]


def _kv_shared(kind, inp):
    d = {}
    if kind == "ffn0":
        d["wkT"] = prep_wT(inp["L0_ffn_k_w"]); d["wvT"] = prep_wT(inp["L0_ffn_v_w"])
        d["kb"] = prep_bias(inp["L0_ffn_k_b"])
        d["vb_row"] = np.ascontiguousarray(np.asarray(inp["L0_ffn_v_b"], np.float32).reshape(1, D))
    elif kind == "attn0":
        ww = np.asarray(inp["L0_attn_W_w"], np.float32)
        wb = np.asarray(inp["L0_attn_W_b"], np.float32)
        d["wkT"] = prep_wT(ww[D:2 * D]); d["wvT"] = prep_wT(ww[2 * D:3 * D])
        d["kb"] = prep_bias(wb[D:2 * D])
        d["vb_row"] = np.ascontiguousarray(wb[2 * D:3 * D].reshape(1, D))
    else:
        d["wkT"] = prep_wT(inp["L1_ffn_k_w"]); d["wvT"] = prep_wT(inp["L1_ffn_v_w"])
        d["kb"] = prep_bias(inp["L1_ffn_k_b"])
        d["vb_row"] = np.ascontiguousarray(np.asarray(inp["L1_ffn_v_b"], np.float32).reshape(1, D))
        d["nw"] = prep_bias(inp["L1_ffn_norm_w"]); d["nb"] = prep_bias(inp["L1_ffn_norm_b"])
    d["ones_col_f32"] = np.ones((128, 1), np.float32)
    d["ones_row_f32"] = np.ones((1, 128), np.float32)
    d["ones_col_bf"] = np.ones((128, 1), bf)
    h2 = np.zeros((128, 2), bf)
    h2[0:64, 0] = 1; h2[64:128, 1] = 1
    d["onesh"] = h2
    return d


def _run_kv(kind, x, kvsh):
    nc = _prog("kv_" + kind, build_kv, kind=kind)
    ims = []
    for c in range(NCORE):
        dd = dict(kvsh)
        dd["xo"] = shard_x(x, c)
        if kind == "ffn0":
            cl, sl_ = rope_tables(local_positions(c))
            dd["rc_glob"], dd["rs_glob"] = cl, sl_
        ims.append(dd)
    res = run_bass_kernel_spmd(nc, ims, core_ids=list(range(NCORE)))
    dh = 128 if kind == "ffn1" else 64
    nh = 8 if kind == "ffn1" else 16
    kfull, v2full = gather_kv([r["kloc"] for r in res.results],
                              [r["v2loc"] for r in res.results], dh, nh)
    kn8 = np.ascontiguousarray(
        np.concatenate([np.asarray(r["kn2out"]) for r in res.results], axis=1))
    return kfull, v2full, kn8


def kernel(**inputs):
    x = np.ascontiguousarray(np.asarray(inputs["L0_attn_x"], np.float32))

    sh_ffn0 = host_shared_ffn0(inputs)
    sh_attn0 = host_shared_attn0(inputs)
    sh_ffn1 = host_shared_ffn1(inputs)
    sh_attn1 = host_shared_attn1(inputs)
    for d in (sh_ffn0, sh_attn0, sh_ffn1):
        for k in ("wkT", "wvT", "kwT", "vwT", "kb", "kb2", "vb_row", "rc_glob", "rs_glob"):
            d.pop(k, None)
    kv_f0 = _kv_shared("ffn0", inputs)
    kv_a0 = _kv_shared("attn0", inputs)
    kv_f1 = _kv_shared("ffn1", inputs)

    nc_ffn0 = _prog("ffn0", build_ffn0, ext_kv=True)
    nc_attn0 = _prog("attn0", build_attn0, ext_kv=True)
    nc_ffn1 = _prog("ffn1", build_ffn1, ext_kv=True)
    nc_attn1 = _prog("attn1", build_attn1)

    skj_per_core = [build_skj(np.asarray(inputs["L0_attn_rpb"], np.float32), c)
                    for c in range(NCORE)]
    rloc_per_core = []
    for c in range(NCORE):
        cl, sl_ = rope_tables(local_positions(c))
        rloc_per_core.append({"rc_loc": cl[0:64].astype(bf), "rs_loc": sl_[0:64].astype(bf)})

    def main_launch(nc, shared, x, extra_per_core=None, kv=None):
        ims = []
        for c in range(NCORE):
            dd = dict(shared)
            dd["xo"] = shard_x(x, c)
            if kv is not None:
                dd["kdram"], dd["v2dram"], dd["kn8"] = kv
            if extra_per_core is not None:
                dd.update(extra_per_core[c])
            ims.append(dd)
        res = run_bass_kernel_spmd(nc, ims, core_ids=list(range(NCORE)))
        return unshard_x([r["xout"] for r in res.results])

    kv = _run_kv("ffn0", x, kv_f0)
    x = main_launch(nc_ffn0, sh_ffn0, x, kv=kv, extra_per_core=rloc_per_core)
    kv = _run_kv("attn0", x, kv_a0)
    x = main_launch(nc_attn0, sh_attn0, x, kv=kv,
                    extra_per_core=[{"skj": skj_per_core[c]} for c in range(NCORE)])
    kv = _run_kv("ffn0", x, kv_f0)
    x = main_launch(nc_ffn0, sh_ffn0, x, kv=kv, extra_per_core=rloc_per_core)
    kv = _run_kv("ffn1", x, kv_f1)
    x = main_launch(nc_ffn1, sh_ffn1, x, kv=kv)
    x = main_launch(nc_attn1, sh_attn1, x)
    kv = _run_kv("ffn1", x, kv_f1)
    x = main_launch(nc_ffn1, sh_ffn1, x, kv=kv)
    return x.astype(np.float32)
